# version 06
# baseline (speedup 1.0000x reference)
import numpy as np
import ml_dtypes
from contextlib import ExitStack

import concourse.bass as bass
import concourse.mybir as mybir
from concourse.bass import ds, IndirectOffsetOnAxis
from concourse.bass_utils import run_bass_kernel_spmd
from concourse.library_overlay import lower_extended_insts

F32 = mybir.dt.float32
BF16 = mybir.dt.bfloat16
U32 = mybir.dt.uint32
AF = mybir.ActivationFunctionType
ALU = mybir.AluOpType
AX = mybir.AxisListType

NCORE = 8
B, T, D, NH, DH = 16, 2048, 1024, 16, 64
NB = 8
TOK = NB * T
TALL = B * T
E, DFF = 8, 4096
CAP = 10240.0
EPS = 1.1920929e-07
BUCKET = 4096
SENT = E * BUCKET
CHUNK = 512
VAL_ROUNDS = 42
IDX_ROUNDS = 17
MAGIC = 0x5A17CA3E

DEBUG_STAGE = None


class CS:
    def __init__(self, nc, name):
        self.h = nc.alloc_semaphore(name=name)
        self.n = 0

    def inc(self, instr, k=1):
        instr.then_inc(self.h, k)
        self.n += k
        return instr


def _consts():
    c = {}
    half = DH // 2
    freqs = np.arange(half, dtype=np.float32) / half
    theta = (1.0 / (10000.0 ** freqs)).astype(np.float32)
    t = np.arange(T, dtype=np.float32)
    ang = t[:, None] * theta[None, :]
    c["ropec"] = np.cos(ang).astype(np.float32)
    c["ropes"] = np.sin(ang).astype(np.float32)
    c["ident"] = np.eye(128, dtype=np.float32)
    ps = np.arange(128)[:, None]
    ft = np.arange(512)[None, :]
    m = np.zeros((128, 4, 512), dtype=np.float32)
    for j in range(4):
        m[:, j, :] = (ps <= ft - 128 * j).astype(np.float32)
    c["masks"] = m.reshape(128, 2048)
    c["onesr"] = np.ones((1, 128), dtype=np.float32)
    c["tri"] = (np.arange(128)[:, None] < np.arange(128)[None, :]).astype(np.float32)
    p = np.arange(128, dtype=np.float32)[:, None]
    gg = np.arange(256, dtype=np.float32)[None, :]
    tRT = (gg * 128.0 + p).astype(np.float32)
    c["tRT"] = tRT
    c["io8"] = np.broadcast_to(np.arange(8, dtype=np.float32), (128, 8)).copy()
    c["io8m"] = (c["io8"] - 999.0).copy()
    idc = np.concatenate([2.0 * tRT, 2.0 * tRT + 1.0], axis=1).astype(np.float32)
    c["idc"] = idc
    c["idcm"] = (idc - 131072.0).copy()
    kp = np.zeros((128, 64), dtype=np.float32)
    kp[:, 32:] = 1.0
    c["kpat"] = kp
    pidx = np.zeros((128, 64), dtype=np.float32)
    pp_ = np.arange(128)[:, None]
    gl_ = np.arange(32)[None, :]
    pidx[:, 0:32] = 2.0 * (gl_ * 128 + pp_) + 0.0
    pidx[:, 32:64] = 2.0 * (gl_ * 128 + pp_) + 1.0
    c["idxpat"] = pidx
    return c


def build(dbg=None):
    nc = bass.Bass("TRN2", target_bir_lowering=False, debug=False, num_devices=8,
                   monotonic_sem_count=5)
    CST = _consts()

    xh = nc.dram_tensor("xh", [TOK, D], F32, kind="ExternalInput").ap()
    wqkv = nc.dram_tensor("wqkv", [D, 3 * D], F32, kind="ExternalInput").ap()
    wo = nc.dram_tensor("wo", [D, D], F32, kind="ExternalInput").ap()
    rw = nc.dram_tensor("rw", [E, D], F32, kind="ExternalInput").ap()
    w1t = nc.dram_tensor("w1t", [E * D, DFF], F32, kind="ExternalInput").ap()
    wgt = nc.dram_tensor("wgt", [E * D, DFF], F32, kind="ExternalInput").ap()
    w2t = nc.dram_tensor("w2t", [E * DFF, D], F32, kind="ExternalInput").ap()
    prm = nc.dram_tensor("prm", [1, 4], U32, kind="ExternalInput").ap()
    prmf = nc.dram_tensor("prmf", [1, 4], F32, kind="ExternalInput").ap()
    out = nc.dram_tensor("out", [4096, D], F32, kind="ExternalOutput").ap()

    xbf_d = nc.dram_tensor("xbf_d", [T, D], BF16).ap()
    qk_d = nc.dram_tensor("qk_d", [T, 2 * D], BF16).ap()
    yt_d = nc.dram_tensor("yt_d", [D, T], BF16).ap()
    mr_d = nc.dram_tensor("mr_d", [1, 4096], BF16).ap()
    rt_d = nc.dram_tensor("rt_d", [128, 256, 4], F32).ap()
    x2o_d = nc.dram_tensor("x2o_d", [4096, D], F32).ap()
    chk_d = nc.dram_tensor("chk_d", [CHUNK, D], BF16).ap()
    x2_d = nc.dram_tensor("x2_d", [TOK, D], F32).ap()
    ln2_d = nc.dram_tensor("ln2_d", [TOK, D], BF16).ap()
    lg_d = nc.dram_tensor("lg_d", [TOK, E], F32).ap()
    bkt_d = nc.dram_tensor("bkt_d", [SENT + 1, D], BF16).ap()
    ob_d = nc.dram_tensor("ob_d", [SENT + 1, D], BF16).ap()
    shLG = nc.dram_tensor("shLG", [TALL, E], F32, addr_space="Shared").ap()
    shFL = nc.dram_tensor("shFL", [2, 64], U32, addr_space="Shared").ap()

    dbg_outs = {}
    if dbg == "attn":
        dbg_outs["d_x2"] = nc.dram_tensor("d_x2", [TOK, D], F32, kind="ExternalOutput").ap()
        dbg_outs["d_lg"] = nc.dram_tensor("d_lg", [TOK, E], F32, kind="ExternalOutput").ap()
        dbg_outs["d_qk"] = nc.dram_tensor("d_qk", [T, 2 * D], F32, kind="ExternalOutput").ap()
        dbg_outs["d_yt"] = nc.dram_tensor("d_yt", [D, T], F32, kind="ExternalOutput").ap()
        dbg_outs["d_xbf"] = nc.dram_tensor("d_xbf", [T, D], F32, kind="ExternalOutput").ap()
        dbg_outs["d_vt"] = nc.dram_tensor("d_vt", [128, 16 * 1040], F32, kind="ExternalOutput").ap()
    if dbg == "route":
        dbg_outs["d_rt"] = nc.dram_tensor("d_rt", [128, 4608], F32, kind="ExternalOutput").ap()
        dbg_outs["d_row"] = nc.dram_tensor("d_row", [16, 64], F32, kind="ExternalOutput").ap()
        dbg_outs["d_lgg"] = nc.dram_tensor("d_lgg", [128, 2048], F32, kind="ExternalOutput").ap()
    if dbg == "moe":
        dbg_outs["d_ob"] = nc.dram_tensor("d_ob", [8 * 1536, D], BF16, kind="ExternalOutput").ap()
        dbg_outs["d_bk"] = nc.dram_tensor("d_bk", [8 * 1536, D], BF16, kind="ExternalOutput").ap()
        dbg_outs["d_lgg"] = nc.dram_tensor("d_lgg", [TALL, E], F32, kind="ExternalOutput").ap()

    def inl(name, arr, dt=F32):
        a = np.ascontiguousarray(arr.astype(ml_dtypes.bfloat16) if dt == BF16 else
                                 arr.astype(np.float32))
        return nc.inline_tensor(a, name=name).ap()

    ropec_c = inl("ropec", CST["ropec"])
    ropes_c = inl("ropes", CST["ropes"])
    ident_c = inl("identb", CST["ident"], BF16)
    identf_c = inl("identf", CST["ident"])
    masks_c = inl("masks", CST["masks"], BF16)
    onesr_c = inl("onesrb", CST["onesr"], BF16)
    onesrf_c = inl("onesrf", CST["onesr"])
    tri_c = inl("trib", CST["tri"], BF16)
    tRT_c = inl("tRT", CST["tRT"])
    io8_c = inl("io8", CST["io8"])
    io8m_c = inl("io8m", CST["io8m"])
    idc_c = inl("idc", CST["idc"])
    idcm_c = inl("idcm", CST["idcm"])
    kpat_c = inl("kpat", CST["kpat"])
    idxpat_c = inl("idxpat", CST["idxpat"])

    g = nc.gpsimd
    sy = nc.sync
    pe = nc.tensor
    ve = nc.vector
    ac = nc.scalar
    MM = pe.matmul

    sG = CS(nc, "sG")
    sH = CS(nc, "sH")
    sP = CS(nc, "sP")
    sV = CS(nc, "sV")
    sA = CS(nc, "sA")
    sO = CS(nc, "sO")
    ms_poll = nc.monotonic_semaphore(0, engine=nc.gpsimd)
    ms_mg = nc.monotonic_semaphore(1, engine=nc.gpsimd)
    ms_mv = nc.monotonic_semaphore(2, engine=nc.vector)
    ms_mv2 = nc.monotonic_semaphore(3, engine=nc.vector)
    ms_wt = nc.monotonic_semaphore(4, engine=nc.gpsimd)

    psA = nc.alloc_psum_tensor("psA", [128, 2048], F32)
    psW = nc.alloc_psum_tensor("psW", [128, 1024], F32)
    psV = nc.alloc_psum_tensor("psV", [128, 512], F32)
    psT = nc.alloc_psum_tensor("psT", [128, 1024], BF16)

    sbl = nc.alloc_sbuf_tensor
    c_ropec = sbl("c_ropec", [128, 16, 32], F32)
    c_ropes = sbl("c_ropes", [128, 16, 32], F32)
    c_ident = sbl("c_ident", [128, 128], BF16)
    c_identf = sbl("c_identf", [128, 128], F32)
    c_masks = sbl("c_masks", [128, 2048], BF16)
    c_onesr = sbl("c_onesr", [1, 128], BF16)
    c_onesrf = sbl("c_onesrf", [1, 128], F32)
    c_onescf = sbl("c_onescf", [128, 1], F32)
    c_tri = sbl("c_tri", [128, 128], BF16)
    c_tRT = sbl("c_tRT", [128, 256], F32)
    c_io8 = sbl("c_io8", [128, 8], F32)
    c_io8m = sbl("c_io8m", [128, 8], F32)
    c_idc = sbl("c_idc", [128, 512], F32)
    c_idcm = sbl("c_idcm", [128, 512], F32)
    c_kpat = sbl("c_kpat", [128, 64], F32)
    c_idxp = sbl("c_idxp", [128, 64], F32)
    prmft = sbl("prmft", [1, 4], F32)
    onescol = sbl("onescol", [128, 1], BF16)
    c_eps = sbl("c_eps", [128, 1], F32)
    zrow = sbl("zrow", [1, 1024], BF16)
    prmt = sbl("prmt", [1, 4], U32)
    flagw = sbl("flagw", [1, 1], U32)
    flrd = sbl("flrd", [1, 1], U32)
    rwT = sbl("rwT", [128, 8, 8], BF16)
    rwbf = sbl("rwbf", [8, 1024], BF16)
    kwsb = sbl("kwsb", [128, 64], F32)
    gdU = sbl("gdU", [128, 64], U32)
    kcu = sbl("kcu", [1, 8], U32)

    sG.inc(g.dma_start(out=prmt[:, :], in_=prm[:, :]), 16)
    for apdst, apsrc in [(c_ropec[:, :, :], ropec_c.rearrange("(k p) f -> p k f", p=128)),
                         (c_ropes[:, :, :], ropes_c.rearrange("(k p) f -> p k f", p=128)),
                         (c_ident[:, :], ident_c[:, :]), (c_identf[:, :], identf_c[:, :]),
                         (c_masks[:, :], masks_c[:, :]), (c_onesr[:, :], onesr_c[:, :]),
                         (c_onesrf[:, :], onesrf_c[:, :]), (c_tri[:, :], tri_c[:, :]),
                         (c_tRT[:, :], tRT_c[:, :]), (c_io8[:, :], io8_c[:, :]),
                         (c_io8m[:, :], io8m_c[:, :]), (c_idc[:, :], idc_c[:, :]),
                         (c_idcm[:, :], idcm_c[:, :]), (c_kpat[:, :], kpat_c[:, :]),
                         (c_idxp[:, :], idxpat_c[:, :]), (prmft[:, :], prmf[:, :])]:
        sH.inc(sy.dma_start(out=apdst, in_=apsrc), 16)
    sG.inc(g.dma_start(out=rwbf[:, :], in_=rw[:, :]), 16)
    g.wait_ge(sG.h, sG.n)
    ve.wait_ge(sG.h, sG.n)
    ve.wait_ge(sH.h, sH.n)
    sV.inc(ve.memset(onescol[:, :], 1.0), 1)
    sV.inc(ve.memset(c_onescf[:, :], 1.0), 1)
    sV.inc(ve.memset(zrow[:, :], 0.0), 1)
    sV.inc(ve.memset(flagw[:, :], MAGIC), 1)
    sV.inc(ve.memset(c_eps[:, :], 1e-5), 1)
    nc.const_aps.register(F32, 1e-5, c_eps[:, 0:1])
    ve.wait_ge(sV.h, sV.n)
    sG.inc(g.dma_start(out=ob_d[SENT:SENT + 1, :], in_=zrow[:, :]), 16)
    pe.wait_ge(sV.h, sV.n)
    for k in range(8):
        sP.inc(pe.transpose(psT[:, 0:8], rwbf[0:8, k * 128:(k + 1) * 128], c_ident[0:8, 0:8]), 1)
        ve.wait_ge(sP.h, sP.n)
        sV.inc(ve.tensor_copy(rwT[:, k, :], psT[:, 0:8]), 1)
        pe.wait_ge(sV.h, sV.n)
    g.wait_ge(sG.h, sG.n)
    r_xbase = nc.values_load(prmt[0:1, 0:1], engines=mybir.ALL_ENGINES, min_val=0, max_val=TALL - TOK)
    r_par = nc.values_load(prmt[0:1, 2:3], engines=mybir.ALL_ENGINES, min_val=0, max_val=1)
    nc.all_engine_barrier()

    att = ExitStack()

    def sba(name, shape, dt):
        return att.enter_context(nc.sbuf_tensor(name, shape, dt))

    woT = sba("woT", [128, 8, 2, 512], BF16)
    xa = sba("xa", [128, 2, 1024], F32)
    xbfb = sba("xbfb", [128, 2, 1024], BF16)
    stat = sba("stat", [128, 8, 16], F32)
    statT = sba("statT", [32, 128], F32)
    mrow = sba("mrow", [1, 2048], BF16)
    rrow = sba("rrow", [1, 2048], BF16)
    scr = sba("scr", [128, 2048], BF16)
    scf = sba("scf", [128, 1024], F32)
    regW = sba("regW", [128, 2, 16, 512], BF16)
    wq_done = {}

    def issue_qkv_pass(pg):
        pss_ = pg % 3
        for nn in range(2):
            for k in range(8):
                n = pss_ * 2 + nn
                sG.inc(g.dma_start(out=regW[:, pg % 2, nn * 8 + k, :],
                                   in_=wqkv[k * 128:(k + 1) * 128, n * 512:(n + 1) * 512]), 16)
        wq_done[pg] = sG.n
    lnT = sba("lnT", [128, 8, 2048], BF16)
    vt = sba("vt", [128, 16, 1040], BF16)
    qkstg = sba("qkstg", [128, 2, 512], BF16)
    qkro = sba("qkro", [128, 2, 1024], BF16)
    qkn = sba("qkn", [128, 2, 64], F32)
    bndrow = sba("bndrow", [1, 64], F32)
    nbias = sba("nbias", [128, 16], F32)
    PTb = sba("PTb", [128, 2, 2048], BF16)
    PTm = sba("PTm", [128, 2048], BF16)
    lbc = sba("lbc", [64, 512], F32)
    linv = sba("linv", [1, 512], F32)
    yTq = sba("yTq", [128, 2, 2048], BF16)
    x2t = sba("x2t", [128, 2, 1024], F32)
    ln2t = sba("ln2t", [128, 2, 1024], BF16)
    lgst = sba("lgst", [128, 16, 8], F32)

    for k in range(8):
        sG.inc(g.dma_start(out=woT[:, k, :, :],
                           in_=wo[k * 128:(k + 1) * 128, :].rearrange("p (n f) -> p n f", n=2)), 16)
    issue_qkv_pass(0)

    for b in range(NB):
        xrow = b * T
        for k in range(16):
            kb = k % 2
            if k >= 2:
                sy.wait_ge(sA.h, sA.n - 2)
            sH.inc(sy.dma_start(out=xa[:, kb, :], in_=xh[xrow + k * 128:xrow + (k + 1) * 128, :]), 16)
            ac.wait_ge(sH.h, sH.n)
            if k >= 2:
                ac.wait_ge(sG.h, sG.n)
            sA.inc(ac.activation(xbfb[:, kb, :], xa[:, kb, :], AF.Identity,
                                 accum_out=stat[:, 0, k:k + 1]), 1)
            sA.inc(ac.activation(scr[:, 0:1024], xa[:, kb, :], AF.Square,
                                 accum_out=stat[:, 1, k:k + 1]), 1)
            g.wait_ge(sA.h, sA.n - 1)
            sG.inc(g.dma_start(out=xbf_d[k * 128:(k + 1) * 128, :], in_=xbfb[:, kb, :]), 16)
        ve.wait_ge(sA.h, sA.n)
        sV.inc(ve.tensor_scalar(stat[:, 2, :], stat[:, 0, :], -1.0 / D, None, ALU.mult), 1)
        sV.inc(ve.tensor_scalar(stat[:, 3, :], stat[:, 1, :], 1.0 / D, None, ALU.mult), 1)
        ve.wait_ge(sV.h, sV.n)
        sV.inc(ve.tensor_mul(stat[:, 4, :], stat[:, 2, :], stat[:, 2, :]), 1)
        ve.wait_ge(sV.h, sV.n)
        sV.inc(ve.tensor_sub(stat[:, 5, :], stat[:, 3, :], stat[:, 4, :]), 1)
        ac.wait_ge(sV.h, sV.n)
        sA.inc(ac.activation(stat[:, 6, :], stat[:, 5, :], AF.Ln, bias=1e-5), 1)
        ac.wait_ge(sA.h, sA.n)
        sA.inc(ac.activation(stat[:, 3, :], stat[:, 6, :], AF.Exp, scale=-0.5), 1)
        pe.wait_ge(sA.h, sA.n)
        sP.inc(pe.transpose(psW[0:32, 0:128], stat[:, 2:4, :].rearrange("p a bb -> p (a bb)"),
                            c_identf[:, :]), 1)
        ve.wait_ge(sP.h, sP.n)
        sV.inc(ve.tensor_copy(statT[:, :], psW[0:32, 0:128]), 1)
        g.wait_ge(sV.h, sV.n)
        sG.inc(g.dma_start(out=mr_d.rearrange("a (r p) -> (a r) p", p=128), in_=statT[:, :]), 16)
        g.wait_ge(sG.h, sG.n)
        sG.inc(g.dma_start(out=mrow[:, :], in_=mr_d[0:1, 0:2048]), 16)
        sG.inc(g.dma_start(out=rrow[:, :], in_=mr_d[0:1, 2048:4096]), 16)
        g.wait_ge(sG.h, sG.n)
        nc.all_engine_barrier()

        for q in range(8):
            if q >= 1:
                sy.wait_ge(sV.h, sV.n)
            sH.inc(sy.dma_start(out=scr[:, :], in_=xbf_d[:, q * 128:(q + 1) * 128],
                                transpose=True), 16)
            for h2 in range(2):
                sl = slice(h2 * 1024, (h2 + 1) * 1024)
                pe.wait_ge(sH.h, sH.n)
                pe.wait_ge(sV.h, sV.n)
                for hh in range(2):
                    s2 = slice(h2 * 1024 + hh * 512, h2 * 1024 + (hh + 1) * 512)
                    sP.inc(MM(psW[:, hh * 512:(hh + 1) * 512], c_onesr[0:1, :],
                              mrow[0:1, s2], start=True, stop=True), 1)
                    sP.inc(MM(psA[:, hh * 512:(hh + 1) * 512], c_onesr[0:1, :],
                              rrow[0:1, s2], start=True, stop=True), 1)
                ve.wait_ge(sP.h, sP.n)
                ve.wait_ge(sH.h, sH.n)
                sV.inc(ve.tensor_tensor(scf[:, :], scr[:, sl], psW[:, :], ALU.add), 1)
                ve.wait_ge(sV.h, sV.n)
                sV.inc(ve.tensor_tensor(lnT[:, q, sl], scf[:, :], psA[:, 0:1024], ALU.mult), 1)
                ve.wait_ge(sV.h, sV.n)
        nc.all_engine_barrier()

        sV.inc(ve.memset(qkn[:, :, :], 0.0), 1)
        sV.inc(ve.memset(vt.ap().rearrange("p t (h f) -> p t h f", f=65)[:, :, :, 64:65], 1.0), 1)
        ve.wait_ge(sV.h, sV.n)
        nc.all_engine_barrier()
        for pss in range(3):
            pg = b * 3 + pss
            if pg + 1 < NB * 3:
                issue_qkv_pass(pg + 1)
            pe.wait_ge(sG.h, wq_done[pg])
            for t_ in range(16):
                qa = t_ % 2
                for nn in range(2):
                    m = t_ * 2 + nn
                    j = m % 4
                    if m >= 4:
                        pe.wait_ge(sA.h, sA.n - 3)
                    if nn == 0 and t_ >= 2 and pss < 2:
                        pe.wait_ge(sG.h, sG.n)
                    for k in range(8):
                        sP.inc(MM(psA[:, j * 512:(j + 1) * 512], lnT[:, k, t_ * 128:(t_ + 1) * 128],
                                  regW[:, pg % 2, nn * 8 + k, :], start=(k == 0), stop=(k == 7)), 1)
                    ac.wait_ge(sP.h, sP.n)
                    if pss < 2:
                        coff = nn * 512
                        if m >= 2:
                            ac.wait_ge(sV.h, sV.n - 2)
                        sA.inc(ac.activation(qkstg[:, nn, :], psA[:, j * 512:(j + 1) * 512],
                                             AF.Identity), 1)
                        ve.wait_ge(sA.h, sA.n)
                        s8 = qkstg[:, nn, :].rearrange("p (h f) -> p h f", h=8)
                        dsl = qkro[:, qa, coff:coff + 512].rearrange("p (h f) -> p h f", h=8)
                        cosv = c_ropec[:, t_, :].unsqueeze(1).to_broadcast([128, 8, 32])
                        sinv = c_ropes[:, t_, :].unsqueeze(1).to_broadcast([128, 8, 32])
                        t8 = scr[:, 0:1024].rearrange("p (a h f) -> p a h f", a=4, h=8)
                        sV.inc(ve.tensor_tensor(t8[:, 0], s8[:, :, 0:32], cosv, ALU.mult), 1)
                        sV.inc(ve.tensor_tensor(t8[:, 1], s8[:, :, 32:64], sinv, ALU.mult), 1)
                        sV.inc(ve.tensor_tensor(t8[:, 2], s8[:, :, 0:32], sinv, ALU.mult), 1)
                        sV.inc(ve.tensor_tensor(t8[:, 3], s8[:, :, 32:64], cosv, ALU.mult), 1)
                        ve.wait_ge(sV.h, sV.n)
                        sV.inc(ve.tensor_sub(dsl[:, :, 0:32], t8[:, 0], t8[:, 1]), 1)
                        sV.inc(ve.tensor_add(dsl[:, :, 32:64], t8[:, 2], t8[:, 3]), 1)
                    else:
                        hv = nn * 8
                        vdst = vt.ap().rearrange("p t (h f) -> p t h f", f=65)[:, t_, hv:hv + 8, 0:64]
                        sA.inc(ac.activation(vdst, psA[:, j * 512:(j + 1) * 512], AF.Identity), 1)
                if pss < 2:
                    pp, qq = t_ % 2, 1 - t_ % 2
                    co = 0 if pss == 0 else 32
                    ve.wait_ge(sV.h, sV.n)
                    sV.inc(ve.tensor_mul(scr[:, 0:1024], qkro[:, qa, :], qkro[:, qa, :]), 1)
                    ve.wait_ge(sV.h, sV.n)
                    sV.inc(ve.tensor_reduce(stat[:, 0, :], scr[:, 0:1024].rearrange("p (h f) -> p h f", h=16),
                                            AX.X, ALU.add), 1)
                    ve.wait_ge(sV.h, sV.n)
                    sV.inc(ve.tensor_max(qkn[:, qq, co:co + 16], qkn[:, pp, co:co + 16], stat[:, 0, :]), 1)
                    g.wait_ge(sV.h, sV.n)
                    sG.inc(g.dma_start(out=qk_d[t_ * 128:(t_ + 1) * 128, pss * 1024:(pss + 1) * 1024],
                                       in_=qkro[:, qa, :]), 16)
            g.wait_ge(sG.h, sG.n)
            nc.all_engine_barrier()
        pe.wait_ge(sV.h, sV.n)
        sP.inc(pe.transpose(psW[0:64, 0:128], qkn[:, 0, :], c_identf[:, :]), 1)
        ve.wait_ge(sP.h, sP.n)
        sV.inc(ve.tensor_reduce(stat[0:64, 0, 0:1], psW[0:64, 0:128], AX.X, ALU.max), 1)
        pe.wait_ge(sV.h, sV.n)
        sP.inc(pe.transpose(psW[0:1, 0:64], stat[0:64, 0, 0:1], c_identf[0:64, 0:64]), 1)
        ve.wait_ge(sP.h, sP.n)
        sV.inc(ve.tensor_copy(bndrow[0:1, 0:64], psW[0:1, 0:64]), 1)
        ve.wait_ge(sV.h, sV.n)
        sV.inc(ve.tensor_mul(bndrow[0:1, 16:32], bndrow[0:1, 0:16], bndrow[0:1, 32:48]), 1)
        ac.wait_ge(sV.h, sV.n)
        sA.inc(ac.activation(bndrow[0:1, 48:64], bndrow[0:1, 16:32], AF.Sqrt), 1)
        ve.wait_ge(sA.h, sA.n)
        sV.inc(ve.tensor_scalar(bndrow[0:1, 0:16], bndrow[0:1, 48:64], -0.1275, None, ALU.mult), 1)
        pe.wait_ge(sV.h, sV.n)
        sP.inc(MM(psW[:, 0:16], c_onesrf[0:1, :], bndrow[0:1, 0:16], start=True, stop=True), 1)
        ac.wait_ge(sP.h, sP.n)
        sA.inc(ac.activation(nbias[:, :], psW[:, 0:16], AF.Identity), 1)
        g.wait_ge(sG.h, sG.n)
        ve.wait_ge(sA.h, sA.n)
        nc.all_engine_barrier()

        for Q in range(4):
            for cq in range(2):
                sH.inc(sy.dma_start(out=lnT[:, cq, :],
                                    in_=qk_d[:, Q * 256 + cq * 128:Q * 256 + (cq + 1) * 128],
                                    transpose=True), 16)
                sH.inc(sy.dma_start(out=lnT[:, 2 + cq, :],
                                    in_=qk_d[:, 1024 + Q * 256 + cq * 128:1024 + Q * 256 + (cq + 1) * 128],
                                    transpose=True), 16)
            ve.wait_ge(sH.h, sH.n)
            nc.all_engine_barrier()
            ptbf = PTb.ap().rearrange("p a f -> p (a f)")
            for l in range(4):
                hq = 4 * Q + l
                cq, rb = l // 2, (l % 2) * 64
                for tc in range(4):
                    nh = 2 * (tc + 1)
                    e0 = sA.n
                    score_done, exp_done, pv_done, mask_done = {}, {}, {}, {}

                    def emit_scores(j):
                        pb = j % 2
                        pe.wait_ge(sA.h, e0 if j <= 1 else exp_done[j - 2])
                        for si in range(2):
                            s = 2 * j + si
                            sP.inc(MM(psA[:, pb * 1024 + si * 512:pb * 1024 + (si + 1) * 512],
                                      lnT[rb:rb + 64, 2 + cq, s * 128:(s + 1) * 128],
                                      lnT[rb:rb + 64, cq, tc * 512:(tc + 1) * 512],
                                      start=True, stop=True), 1)
                        score_done[j] = sP.n
                        ac.wait_ge(sP.h, score_done[j])
                        if j >= 4:
                            ac.wait_ge(sP.h, pv_done[j - 4])
                        sA.inc(ac.activation(ptbf[:, (j % 4) * 1024:(j % 4 + 1) * 1024],
                                             psA[:, pb * 1024:pb * 1024 + 1024], AF.Exp,
                                             bias=nbias[:, hq:hq + 1], scale=0.125), 1)
                        exp_done[j] = sA.n
                        if j >= nh - 2:
                            dv = j - (nh - 2)
                            ve.wait_ge(sA.h, exp_done[j])
                            sV.inc(ve.tensor_mul(PTm[:, dv * 1024:(dv + 1) * 1024],
                                                 ptbf[:, (j % 4) * 1024:(j % 4 + 1) * 1024],
                                                 c_masks[:, dv * 1024:(dv + 1) * 1024]), 1)
                            mask_done[j] = sV.n

                    def emit_pv(j):
                        pe.wait_ge(sA.h, exp_done[j])
                        if j in mask_done:
                            pe.wait_ge(sV.h, mask_done[j])
                        for si in range(2):
                            s = 2 * j + si
                            if j in mask_done:
                                dv = j - (nh - 2)
                                rhs = PTm[:, dv * 1024 + si * 512:dv * 1024 + (si + 1) * 512]
                            else:
                                rhs = ptbf[:, (j % 4) * 1024 + si * 512:(j % 4) * 1024 + (si + 1) * 512]
                            sP.inc(MM(psV[0:65, :], vt[:, s, hq * 65:(hq + 1) * 65], rhs,
                                      start=(j == 0 and si == 0), stop=(j == nh - 1 and si == 1)), 1)
                        pv_done[j] = sP.n

                    for j in range(nh):
                        emit_scores(j)
                        if j >= 1:
                            emit_pv(j - 1)
                    emit_pv(nh - 1)
                    ve.wait_ge(sP.h, sP.n)
                    sV.inc(ve.reciprocal(linv[0:1, :], psV[64:65, :]), 1)
                    pe.wait_ge(sV.h, sV.n)
                    sP.inc(MM(psW[0:64, 0:512], c_onesrf[0:1, 0:64], linv[0:1, :],
                              start=True, stop=True), 1)
                    ac.wait_ge(sP.h, sP.n)
                    sA.inc(ac.activation(lbc[:, :], psW[0:64, 0:512], AF.Identity), 1)
                    ve.wait_ge(sA.h, sA.n)
                    sV.inc(ve.tensor_mul(yTq[rb:rb + 64, cq, tc * 512:(tc + 1) * 512],
                                         psV[0:64, :], lbc[:, :]), 1)
                    pe.wait_ge(sV.h, sV.n)
            g.wait_ge(sV.h, sV.n)
            for cq in range(2):
                sG.inc(g.dma_start(out=yt_d[Q * 256 + cq * 128:Q * 256 + (cq + 1) * 128, :],
                                   in_=yTq[:, cq, :]), 16)
            g.wait_ge(sG.h, sG.n)
            nc.all_engine_barrier()

        for k in range(8):
            sH.inc(sy.dma_start(out=lnT[:, k, :], in_=yt_d[k * 128:(k + 1) * 128, :]), 16)
        ve.wait_ge(sH.h, sH.n)
        nc.all_engine_barrier()
        g5 = sG.n
        add_done = []
        for t_ in range(16):
            kb = t_ % 2
            tgt = psW[:, 0:1024] if kb == 0 else psA[:, 0:1024]
            if t_ >= 2:
                sy.wait_ge(sV.h, add_done[t_ - 2])
            sH.inc(sy.dma_start(out=xa[:, kb, :], in_=xh[xrow + t_ * 128:xrow + (t_ + 1) * 128, :]), 16)
            if t_ >= 2:
                pe.wait_ge(sV.h, add_done[t_ - 2])
            for k in range(8):
                sP.inc(MM(tgt[:, 0:512], lnT[:, k, t_ * 128:(t_ + 1) * 128], woT[:, k, 0, :],
                          start=(k == 0), stop=(k == 7)), 1)
            for k in range(8):
                sP.inc(MM(tgt[:, 512:1024], lnT[:, k, t_ * 128:(t_ + 1) * 128], woT[:, k, 1, :],
                          start=(k == 0), stop=(k == 7)), 1)
            ve.wait_ge(sP.h, sP.n)
            ve.wait_ge(sH.h, sH.n)
            if t_ >= 2:
                ve.wait_ge(sG.h, g5 + (t_ - 1) * 32)
            sV.inc(ve.tensor_add(x2t[:, kb, :], tgt[:, :], xa[:, kb, :]), 1)
            add_done.append(sV.n)
            g.wait_ge(sV.h, sV.n)
            sG.inc(g.dma_start(out=x2_d[xrow + t_ * 128:xrow + (t_ + 1) * 128, :], in_=x2t[:, kb, :]), 16)
            ac.wait_ge(sV.h, sV.n)
            sA.inc(ac.activation(scr[:, 0:1024], x2t[:, kb, :], AF.Identity, accum_out=stat[:, 0, 0:1]), 1)
            sA.inc(ac.activation(scr[:, 1024:2048], x2t[:, kb, :], AF.Square, accum_out=stat[:, 1, 0:1]), 1)
            ve.wait_ge(sA.h, sA.n)
            sV.inc(ve.tensor_scalar(stat[:, 2, 0:1], stat[:, 0, 0:1], -1.0 / D, None, ALU.mult), 1)
            sV.inc(ve.tensor_scalar(stat[:, 3, 0:1], stat[:, 1, 0:1], 1.0 / D, None, ALU.mult), 1)
            ve.wait_ge(sV.h, sV.n)
            sV.inc(ve.tensor_mul(stat[:, 4, 0:1], stat[:, 2, 0:1], stat[:, 2, 0:1]), 1)
            ve.wait_ge(sV.h, sV.n)
            sV.inc(ve.tensor_sub(stat[:, 5, 0:1], stat[:, 3, 0:1], stat[:, 4, 0:1]), 1)
            ac.wait_ge(sV.h, sV.n)
            sA.inc(ac.activation(stat[:, 6, 0:1], stat[:, 5, 0:1], AF.Ln, bias=1e-5), 1)
            ac.wait_ge(sA.h, sA.n)
            sA.inc(ac.activation(stat[:, 7, 0:1], stat[:, 6, 0:1], AF.Exp, scale=-0.5), 1)
            ve.wait_ge(sA.h, sA.n)
            sV.inc(ve.tensor_scalar(scf[:, :], x2t[:, kb, :], stat[:, 2, 0:1], None, ALU.add), 1)
            ve.wait_ge(sV.h, sV.n)
            sV.inc(ve.tensor_scalar(ln2t[:, kb, :], scf[:, :], stat[:, 7, 0:1], None, ALU.mult), 1)
            g.wait_ge(sV.h, sV.n)
            sG.inc(g.dma_start(out=ln2_d[xrow + t_ * 128:xrow + (t_ + 1) * 128, :], in_=ln2t[:, kb, :]), 16)
        g.wait_ge(sG.h, sG.n)
        nc.all_engine_barrier()
        for k in range(8):
            sH.inc(sy.dma_start(out=lnT[:, k, :], in_=ln2_d[xrow:xrow + 2048, k * 128:(k + 1) * 128],
                                transpose=True), 16)
        ve.wait_ge(sH.h, sH.n)
        nc.all_engine_barrier()
        for t_ in range(16):
            if t_ >= 1:
                ac.wait_ge(sA.h, sA.n)
                pe.wait_ge(sA.h, sA.n)
            for k in range(8):
                sP.inc(MM(psV[:, 0:8], lnT[:, k, t_ * 128:(t_ + 1) * 128], rwT[:, k, :],
                          start=(k == 0), stop=(k == 7)), 1)
            ac.wait_ge(sP.h, sP.n)
            sA.inc(ac.activation(lgst[:, t_, :], psV[:, 0:8], AF.Identity), 1)
        g.wait_ge(sA.h, sA.n)
        sG.inc(g.dma_start(out=lg_d[xrow:xrow + 2048, :].rearrange("(k p) e -> p k e", p=128),
                           in_=lgst[:, :, :]), 16)
        g.wait_ge(sG.h, sG.n)
        nc.all_engine_barrier()

    if dbg == "attn":
        dst = ExitStack()
        dmp = dst.enter_context(nc.sbuf_tensor("dmp", [128, 1024], F32))
        dmp2 = dst.enter_context(nc.sbuf_tensor("dmp2", [128, 8], F32))
        for r in range(TOK // 128):
            sy.wait_ge(sO.h, sO.n)
            sH.inc(sy.dma_start(out=dmp[:, :], in_=x2_d[r * 128:(r + 1) * 128, :]), 16)
            ve.wait_ge(sH.h, sH.n)
            sV.inc(ve.engine_nop(), 1)
            g.wait_ge(sV.h, sV.n)
            sO.inc(g.dma_start(out=dbg_outs["d_x2"][r * 128:(r + 1) * 128, :], in_=dmp[:, :]), 16)
            g.wait_ge(sO.h, sO.n)
            sy.wait_ge(sO.h, sO.n)
            sH.inc(sy.dma_start(out=dmp2[:, :], in_=lg_d[r * 128:(r + 1) * 128, :]), 16)
            ve.wait_ge(sH.h, sH.n)
            sV.inc(ve.engine_nop(), 1)
            g.wait_ge(sV.h, sV.n)
            sO.inc(g.dma_start(out=dbg_outs["d_lg"][r * 128:(r + 1) * 128, :], in_=dmp2[:, :]), 16)
            g.wait_ge(sO.h, sO.n)
        for r in range(16):
            sO.inc(g.dma_start(out=dbg_outs["d_qk"][r * 128:(r + 1) * 128, :],
                               in_=qk_d[r * 128:(r + 1) * 128, :]), 16)
            sO.inc(g.dma_start(out=dbg_outs["d_xbf"][r * 128:(r + 1) * 128, :],
                               in_=xbf_d[r * 128:(r + 1) * 128, :]), 16)
            g.wait_ge(sO.h, sO.n)
        for r in range(8):
            sO.inc(g.dma_start(out=dbg_outs["d_yt"][r * 128:(r + 1) * 128, :],
                               in_=yt_d[r * 128:(r + 1) * 128, :]), 16)
            g.wait_ge(sO.h, sO.n)
        sO.inc(g.dma_start(out=dbg_outs["d_vt"][:, :], in_=vt.ap().rearrange("p t f -> p (t f)")), 16)
        g.wait_ge(sO.h, sO.n)
        dst.close()
        att.close()
        return nc, dbg_outs

    att.close()

    g.wait_ge(sG.h, sG.n)
    sG.inc(g.dma_start(out=shLG[ds(r_xbase, TOK), :], in_=lg_d[:, :]), 16)
    g.wait_ge(sG.h, sG.n)
    sG.inc(g.dma_start(out=shFL[ds(r_par, 1), 0:1], in_=flagw[:, :]), 16)
    g.wait_ge(sG.h, sG.n)
    with g.register("notdone") as regnd:
        g.reg_mov(regnd, 1)
        with g.While(regnd):
            g.dma_start(out=flrd[:, :], in_=shFL[ds(1 - r_par, 1), 0:1]).then_inc(ms_poll.sem(), 16)
            ms_poll.wait_inc(16)
            vfl = g.value_load(flrd[0:1, 0:1])
            g.reg_alu(regnd, vfl, MAGIC, ALU.not_equal)
    nc.all_engine_barrier()

    rt = ExitStack()

    def sbr(name, shape, dt):
        return rt.enter_context(nc.sbuf_tensor(name, shape, dt))

    r_og2 = nc.values_load(prmt[0:1, 3:4], engines=mybir.ALL_ENGINES, min_val=0, max_val=256 - 32)
    r_ox2 = nc.values_load(prmt[0:1, 1:2], engines=mybir.ALL_ENGINES, min_val=0, max_val=TOK - 4096)
    lg = sbr("lg", [128, 256, 8], F32)
    rA = sbr("rA", [128, 2048], F32)
    rB = sbr("rB", [128, 2048], F32)
    rC = sbr("rC", [128, 2048], F32)
    rD = sbr("rD", [128, 2048], F32)
    r6 = sbr("r6", [128, 8, 256], F32)
    we = sbr("we", [128, 8, 512], F32)
    wGT = sbr("wGT", [128, 8, 512], F32)
    wEQ = sbr("wEQ", [128, 8, 512], F32)
    idsel = sbr("idsel", [128, 8, 512], F32)
    colx = sbr("colx", [128, 8], F32)
    rr = sbr("rr", [1, 16, 8], F32)
    thb = sbr("thb", [128, 8], F32)
    Ib = sbr("Ib", [128, 8], F32)
    w12o = sbr("w12o", [128, 64], F32)
    e12o = sbr("e12o", [128, 64], F32)
    idxo = sbr("idxo", [128, 64], F32)
    keep = sbr("keep", [128, 64], F32)
    ohE = sbr("ohE", [128, 64], F32)
    kpe = sbr("kpe", [128, 64], BF16)
    slotp = sbr("slotp", [128, 64], F32)
    gdf = sbr("gdf", [128, 64], F32)
    gacc = sbr("gacc", [128, 2, 64], F32)
    tmp1 = sbr("tmp1", [128, 64], F32)
    tmp2 = sbr("tmp2", [128, 64], F32)
    crow = sbr("crow", [1, 4, 64], F32)
    nrow = sbr("nrow", [1, 64], F32)

    (LO, HI, MID, CNT, GE, GM, D1, D2, M1, M2, TH, GG, TT_, QQ, ILO, IHI) = range(16)

    def R(i):
        return rr[0:1, i, :]

    def vstep(instr):
        sV.inc(instr, 1)
        ve.wait_ge(sV.h, sV.n)

    sH.inc(sy.dma_start(out=lg[:, :, :], in_=shLG.rearrange("(gg p) e -> p gg e", p=128)), 16)
    ve.wait_ge(sH.h, sH.n)

    lgf = lg.ap().rearrange("p gg e -> p (gg e)")

    def bc8(x):
        return x.unsqueeze(2).to_broadcast([128, 256, 8])

    def g3(x2d):
        return x2d.rearrange("p (gg e) -> p gg e", e=8)

    io8b = c_io8.ap().unsqueeze(1).to_broadcast([128, 256, 8])
    io8mb = c_io8m.ap().unsqueeze(1).to_broadcast([128, 256, 8])
    vstep(ve.tensor_reduce(r6[:, 0, :], lg[:, :, :], AX.X, ALU.max))
    vstep(ve.tensor_tensor(g3(rA[:, :]), lg[:, :, :], bc8(r6[:, 0, :]), ALU.subtract))
    ac.wait_ge(sV.h, sV.n)
    sA.inc(ac.activation(rB[:, :], rA[:, :], AF.Exp), 1)
    ve.wait_ge(sA.h, sA.n)
    vstep(ve.tensor_reduce(r6[:, 0, :], rB[:, :].rearrange("p (gg e) -> p gg e", e=8), AX.X, ALU.add))
    vstep(ve.reciprocal(r6[:, 1, :], r6[:, 0, :]))
    vstep(ve.tensor_tensor(g3(rA[:, :]), g3(rB[:, :]), bc8(r6[:, 1, :]), ALU.mult))
    vstep(ve.tensor_reduce(r6[:, 0, :], rA[:, :].rearrange("p (gg e) -> p gg e", e=8), AX.X, ALU.max))
    vstep(ve.tensor_tensor(g3(rB[:, :]), g3(rA[:, :]), bc8(r6[:, 0, :]), ALU.is_ge))
    vstep(ve.tensor_tensor(g3(rC[:, :]), g3(rB[:, :]), io8mb, ALU.mult))
    vstep(ve.tensor_scalar(rD[:, :], rC[:, :], 999.0, None, ALU.add))
    vstep(ve.tensor_reduce(r6[:, 1, :], rD[:, :].rearrange("p (gg e) -> p gg e", e=8), AX.X, ALU.min))
    vstep(ve.tensor_tensor(g3(rB[:, :]), io8b, bc8(r6[:, 1, :]), ALU.is_equal))
    vstep(ve.scalar_tensor_tensor(rC[:, :], rB[:, :], -2.0, rA[:, :], ALU.mult, ALU.add))
    vstep(ve.tensor_reduce(r6[:, 2, :], rC[:, :].rearrange("p (gg e) -> p gg e", e=8), AX.X, ALU.max))
    vstep(ve.tensor_tensor(g3(rB[:, :]), g3(rC[:, :]), bc8(r6[:, 2, :]), ALU.is_ge))
    vstep(ve.tensor_tensor(g3(rD[:, :]), g3(rB[:, :]), io8mb, ALU.mult))
    vstep(ve.tensor_scalar(rB[:, :], rD[:, :], 999.0, None, ALU.add))
    vstep(ve.tensor_reduce(r6[:, 3, :], rB[:, :].rearrange("p (gg e) -> p gg e", e=8), AX.X, ALU.min))
    vstep(ve.tensor_add(r6[:, 4, :], r6[:, 0, :], r6[:, 2, :]))
    vstep(ve.tensor_scalar(r6[:, 5, :], r6[:, 4, :], EPS, None, ALU.max))
    vstep(ve.reciprocal(r6[:, 6, :], r6[:, 5, :]))
    vstep(ve.tensor_tensor(r6[:, 4, :], r6[:, 0, :], r6[:, 6, :], ALU.mult))
    vstep(ve.tensor_tensor(r6[:, 5, :], r6[:, 2, :], r6[:, 6, :], ALU.mult))
    vstep(ve.tensor_copy(rA[:, 0:256], r6[:, 4, :]))
    vstep(ve.tensor_copy(rA[:, 256:512], r6[:, 5, :]))
    for e_ in range(E):
        vstep(ve.tensor_scalar(rB[:, 0:256], r6[:, 1, :], float(e_), None, ALU.is_equal))
        vstep(ve.tensor_scalar(rB[:, 256:512], r6[:, 3, :], float(e_), None, ALU.is_equal))
        vstep(ve.tensor_tensor(rC[:, 0:512], rB[:, 0:512], rA[:, 0:512], ALU.mult))
        vstep(ve.tensor_scalar(rD[:, 0:512], rB[:, 0:512], -1.0, None, ALU.add))
        vstep(ve.tensor_tensor(we[:, e_, :], rC[:, 0:512], rD[:, 0:512], ALU.add))

    wef = we.ap().rearrange("p e c -> p (e c)")
    wgf = wGT.ap().rearrange("p e c -> p (e c)")
    weqf = wEQ.ap().rearrange("p e c -> p (e c)")
    idf = idsel.ap().rearrange("p e c -> p (e c)")

    def bcth(lohalf):
        sl = slice(0, 4) if lohalf == 0 else slice(4, 8)
        return thb.ap()[:, sl].unsqueeze(2).to_broadcast([128, 4, 512])

    def e3(x2d, ne=4):
        return x2d.rearrange("p (e c) -> p e c", c=512)

    def count8(srcf, cmp_op, thbuf=True):
        vstep(ve.tensor_tensor(e3(rA[:, 0:2048]), e3(srcf[:, 0:2048]), bcth(0), cmp_op))
        vstep(ve.tensor_tensor(e3(rB[:, 0:2048]), e3(srcf[:, 2048:4096]), bcth(1), cmp_op))
        vstep(ve.tensor_reduce(colx[:, 0:4], rA[:, 0:2048].rearrange("p (e c) -> p e c", c=512),
                               AX.X, ALU.add))
        vstep(ve.tensor_reduce(colx[:, 4:8], rB[:, 0:2048].rearrange("p (e c) -> p e c", c=512),
                               AX.X, ALU.add))
        pe.wait_ge(sV.h, sV.n)
        sP.inc(MM(psV[0:1, 0:8], c_onescf[:, :], colx[:, :], start=True, stop=True), 1)
        ve.wait_ge(sP.h, sP.n)

    def bcrow(src_row):
        pe.wait_ge(sV.h, sV.n)
        sP.inc(MM(psW[:, 0:8], c_onesrf[0:1, :], src_row, start=True, stop=True), 1)
        ve.wait_ge(sP.h, sP.n)
        vstep(ve.tensor_copy(thb[:, :], psW[:, 0:8]))

    vstep(ve.memset(R(LO), -1.0))
    vstep(ve.memset(R(HI), 1.5))
    for rnd in range(VAL_ROUNDS):
        vstep(ve.tensor_add(R(MID), R(LO), R(HI)))
        vstep(ve.tensor_scalar(R(D1), R(MID), 0.5, None, ALU.mult))
        bcrow(R(D1))
        count8(wef, ALU.is_gt)
        vstep(ve.tensor_copy(R(CNT), psV[0:1, 0:8]))
        vstep(ve.tensor_scalar(R(GE), R(CNT), CAP, None, ALU.is_ge))
        vstep(ve.tensor_sub(R(D2), R(D1), R(LO)))
        vstep(ve.tensor_mul(R(M1), R(GE), R(D2)))
        vstep(ve.tensor_add(R(M2), R(LO), R(M1)))
        vstep(ve.tensor_sub(R(D2), R(D1), R(HI)))
        vstep(ve.tensor_scalar(R(GM), R(GE), -1.0, None, ALU.add))
        vstep(ve.tensor_mul(R(M1), R(GM), R(D2)))
        vstep(ve.tensor_sub(R(D1), R(HI), R(M1)))
        vstep(ve.tensor_copy(R(LO), R(M2)))
        vstep(ve.tensor_copy(R(HI), R(D1)))
    bcrow(R(LO))
    vstep(ve.tensor_tensor(wGT[:, :, :], we[:, :, :], thb.ap().unsqueeze(2).to_broadcast([128, 8, 512]),
                           ALU.is_gt))
    vstep(ve.tensor_scalar(rC[:, 0:2048], wgf[:, 0:2048], -3.0, None, ALU.mult))
    vstep(ve.tensor_scalar(rD[:, 0:2048], wgf[:, 2048:4096], -3.0, None, ALU.mult))
    vstep(ve.tensor_scalar(rA[:, 0:2048], rC[:, 0:2048], 3.0, None, ALU.add))
    vstep(ve.tensor_scalar(rB[:, 0:2048], rD[:, 0:2048], 3.0, None, ALU.add))
    vstep(ve.tensor_tensor(rC[:, 0:2048], rA[:, 0:2048], wef[:, 0:2048], ALU.add))
    vstep(ve.tensor_tensor(rD[:, 0:2048], rB[:, 0:2048], wef[:, 2048:4096], ALU.add))
    vstep(ve.tensor_reduce(colx[:, 0:4], rC[:, 0:2048].rearrange("p (e c) -> p e c", c=512),
                           AX.X, ALU.min))
    vstep(ve.tensor_reduce(colx[:, 4:8], rD[:, 0:2048].rearrange("p (e c) -> p e c", c=512),
                           AX.X, ALU.min))
    pe.wait_ge(sV.h, sV.n)
    sP.inc(pe.transpose(psW[0:8, 0:128], colx[:, :], c_identf[:, :]), 1)
    ve.wait_ge(sP.h, sP.n)
    vstep(ve.tensor_reduce(colx[0:8, 0:1], psW[0:8, 0:128], AX.X, ALU.min))
    pe.wait_ge(sV.h, sV.n)
    sP.inc(pe.transpose(psW[0:1, 0:8], colx[0:8, 0:1], c_identf[0:8, 0:8]), 1)
    ve.wait_ge(sP.h, sP.n)
    vstep(ve.tensor_copy(R(TH), psW[0:1, 0:8]))
    bcrow(R(TH))
    vstep(ve.tensor_tensor(wGT[:, :, :], we[:, :, :], thb.ap().unsqueeze(2).to_broadcast([128, 8, 512]),
                           ALU.is_gt))
    vstep(ve.tensor_tensor(wEQ[:, :, :], we[:, :, :], thb.ap().unsqueeze(2).to_broadcast([128, 8, 512]),
                           ALU.is_equal))
    vstep(ve.tensor_reduce(colx[:, :], wGT[:, :, :], AX.X, ALU.add))
    pe.wait_ge(sV.h, sV.n)
    sP.inc(MM(psV[0:1, 0:8], c_onescf[:, :], colx[:, :], start=True, stop=True), 1)
    ve.wait_ge(sP.h, sP.n)
    vstep(ve.tensor_copy(R(GG), psV[0:1, 0:8]))
    vstep(ve.tensor_reduce(colx[:, :], wEQ[:, :, :], AX.X, ALU.add))
    pe.wait_ge(sV.h, sV.n)
    sP.inc(MM(psV[0:1, 0:8], c_onescf[:, :], colx[:, :], start=True, stop=True), 1)
    ve.wait_ge(sP.h, sP.n)
    vstep(ve.tensor_copy(R(TT_), psV[0:1, 0:8]))
    vstep(ve.tensor_scalar(R(M1), R(GG), -1.0, None, ALU.mult))
    vstep(ve.tensor_scalar(R(M2), R(M1), CAP, None, ALU.add))
    vstep(ve.tensor_tensor(R(QQ), R(M2), R(TT_), ALU.min))
    idcmb4 = c_idcm.ap().unsqueeze(1).to_broadcast([128, 4, 512])
    vstep(ve.tensor_tensor(e3(rA[:, 0:2048]), wEQ[:, 0:4, :], idcmb4, ALU.mult))
    vstep(ve.tensor_tensor(e3(rB[:, 0:2048]), wEQ[:, 4:8, :], idcmb4, ALU.mult))
    vstep(ve.tensor_scalar(idf[:, 0:2048], rA[:, 0:2048], 131072.0, None, ALU.add))
    vstep(ve.tensor_scalar(idf[:, 2048:4096], rB[:, 0:2048], 131072.0, None, ALU.add))
    vstep(ve.memset(R(ILO), -0.5))
    vstep(ve.memset(R(IHI), 65535.5))
    for rnd in range(IDX_ROUNDS):
        vstep(ve.tensor_add(R(MID), R(ILO), R(IHI)))
        vstep(ve.tensor_scalar(R(D1), R(MID), 0.5, None, ALU.mult))
        bcrow(R(D1))
        count8(idf, ALU.is_le)
        vstep(ve.tensor_copy(R(CNT), psV[0:1, 0:8]))
        vstep(ve.tensor_tensor(R(GE), R(CNT), R(QQ), ALU.is_ge))
        vstep(ve.tensor_sub(R(D2), R(D1), R(IHI)))
        vstep(ve.tensor_mul(R(M1), R(GE), R(D2)))
        vstep(ve.tensor_add(R(M2), R(IHI), R(M1)))
        vstep(ve.tensor_sub(R(D2), R(D1), R(ILO)))
        vstep(ve.tensor_scalar(R(GM), R(GE), -1.0, None, ALU.add))
        vstep(ve.tensor_mul(R(M1), R(GM), R(D2)))
        vstep(ve.tensor_sub(R(D1), R(ILO), R(M1)))
        vstep(ve.tensor_copy(R(IHI), R(M2)))
        vstep(ve.tensor_copy(R(ILO), R(D1)))
    bcrow(R(TH))
    pe.wait_ge(sV.h, sV.n)
    sP.inc(MM(psW[:, 0:8], c_onesrf[0:1, :], R(IHI), start=True, stop=True), 1)
    ve.wait_ge(sP.h, sP.n)
    vstep(ve.tensor_copy(Ib[:, :], psW[:, 0:8]))

    rQ = rD.ap()[:, 0:1024].rearrange("p (gl q) -> p gl q", q=4)
    for jj, srcap in ((0, r6[:, 4, :]), (1, r6[:, 5, :]), (2, r6[:, 1, :]), (3, r6[:, 3, :])):
        vstep(ve.tensor_copy(rQ[:, :, jj], srcap))
    g.wait_ge(sV.h, sV.n)
    sG.inc(g.dma_start(out=rt_d[:, :, :], in_=rQ), 16)
    g.wait_ge(sG.h, sG.n)
    sG.inc(g.dma_start(out=tmp1.ap().rearrange("p (gl q) -> p gl q", q=2).unsqueeze(3) if False else
                       gacc[:, :, :].rearrange("p a c -> p (a c)").rearrange("p (gl q) -> p gl q", q=4),
                       in_=rt_d[:, ds(r_og2, 32), :]), 16)
    g.wait_ge(sG.h, sG.n)
    ve.wait_ge(sG.h, sG.n)
    gq = gacc.ap().rearrange("p a c -> p (a c)").rearrange("p (gl q) -> p gl q", q=4)
    vstep(ve.tensor_copy(w12o[:, 0:32], gq[:, :, 0]))
    vstep(ve.tensor_copy(w12o[:, 32:64], gq[:, :, 1]))
    vstep(ve.tensor_copy(e12o[:, 0:32], gq[:, :, 2]))
    vstep(ve.tensor_copy(e12o[:, 32:64], gq[:, :, 3]))
    pe.wait_ge(sV.h, sV.n)
    sP.inc(MM(psV[:, 0:1], c_onesrf[0:1, :], prmft[0:1, 1:2], start=True, stop=True), 1)
    ve.wait_ge(sP.h, sP.n)
    vstep(ve.tensor_copy(tmp2[:, 0:1], psV[:, 0:1]))
    vstep(ve.tensor_scalar(idxo[:, :], c_idxp[:, :], tmp2[:, 0:1], None, ALU.add))
    vstep(ve.memset(gacc[:, 0, :], 0.0))
    vstep(ve.memset(gacc[:, 1, :], 0.0))
    for e_ in range(E):
        vstep(ve.tensor_scalar(ohE[:, :], e12o[:, :], float(e_), None, ALU.is_equal))
        vstep(ve.tensor_tensor(tmp1[:, :], ohE[:, :],
                               thb.ap()[:, e_:e_ + 1].to_broadcast([128, 64]), ALU.mult))
        vstep(ve.tensor_add(tmp2[:, :], gacc[:, 0, :], tmp1[:, :]))
        vstep(ve.tensor_copy(gacc[:, 0, :], tmp2[:, :]))
        vstep(ve.tensor_tensor(tmp1[:, :], ohE[:, :],
                               Ib.ap()[:, e_:e_ + 1].to_broadcast([128, 64]), ALU.mult))
        vstep(ve.tensor_add(tmp2[:, :], gacc[:, 1, :], tmp1[:, :]))
        vstep(ve.tensor_copy(gacc[:, 1, :], tmp2[:, :]))
    vstep(ve.tensor_tensor(tmp1[:, :], w12o[:, :], gacc[:, 0, :], ALU.is_gt))
    vstep(ve.tensor_tensor(tmp2[:, :], w12o[:, :], gacc[:, 0, :], ALU.is_equal))
    vstep(ve.tensor_tensor(slotp[:, :], idxo[:, :], gacc[:, 1, :], ALU.is_le))
    vstep(ve.tensor_tensor(gdf[:, :], tmp2[:, :], slotp[:, :], ALU.mult))
    vstep(ve.tensor_max(keep[:, :], tmp1[:, :], gdf[:, :]))
    vstep(ve.tensor_mul(tmp1[:, :], w12o[:, :], keep[:, :]))
    vstep(ve.tensor_add(tmp2[:, 0:32], tmp1[:, 0:32], tmp1[:, 32:64]))
    vstep(ve.tensor_scalar(tmp2[:, 32:64], tmp2[:, 0:32], EPS, None, ALU.max))
    vstep(ve.reciprocal(slotp[:, 0:32], tmp2[:, 32:64]))
    vstep(ve.tensor_tensor(kwsb.ap().rearrange("p (a bb) -> p a bb", a=2),
                           tmp1.ap().rearrange("p (a bb) -> p a bb", a=2),
                           slotp[:, 0:32].unsqueeze(1).to_broadcast([128, 2, 32]),
                           ALU.mult))
    vstep(ve.memset(gdf[:, :], 0.0))
    for e_ in range(E):
        vstep(ve.tensor_scalar(ohE[:, :], e12o[:, :], float(e_), None, ALU.is_equal))
        vstep(ve.tensor_mul(tmp1[:, :], ohE[:, :], keep[:, :]))
        vstep(ve.tensor_copy(kpe[:, :], tmp1[:, :]))
        pe.wait_ge(sV.h, sV.n)
        sP.inc(MM(psW[:, 0:64], c_tri[:, :], kpe[:, :], start=True, stop=True), 1)
        sP.inc(MM(psV[0:1, 0:64], onescol[:, :], kpe[:, :], start=True, stop=True), 1)
        ve.wait_ge(sP.h, sP.n)
        vstep(ve.tensor_copy(slotp[:, :], psW[:, 0:64]))
        vstep(ve.tensor_copy(crow[0:1, 0, :], psV[0:1, 0:64]))
        vstep(ve.tensor_copy(crow[0:1, 2, :], psV[0:1, 0:64]))
        for sh in (1, 2, 4, 8, 16, 32):
            vstep(ve.tensor_add(crow[0:1, 1, sh:64], crow[0:1, 0, sh:64], crow[0:1, 0, 0:64 - sh]))
            vstep(ve.tensor_copy(crow[0:1, 1, 0:sh], crow[0:1, 0, 0:sh]))
            vstep(ve.tensor_copy(crow[0:1, 0, :], crow[0:1, 1, :]))
        vstep(ve.tensor_copy(nrow[0:1, e_:e_ + 1], crow[0:1, 0, 63:64]))
        vstep(ve.tensor_sub(crow[0:1, 1, :], crow[0:1, 0, :], crow[0:1, 2, :]))
        pe.wait_ge(sV.h, sV.n)
        sP.inc(MM(psW[:, 0:64], c_onesrf[0:1, :], crow[0:1, 1, :], start=True, stop=True), 1)
        ve.wait_ge(sP.h, sP.n)
        vstep(ve.tensor_tensor(tmp1[:, :], slotp[:, :], psW[:, 0:64], ALU.add))
        vstep(ve.tensor_scalar(tmp2[:, :], tmp1[:, :], float(e_ * BUCKET), None, ALU.add))
        vstep(ve.tensor_tensor(tmp1[:, :], tmp2[:, :], kpe[:, :], ALU.mult))
        vstep(ve.tensor_add(tmp2[:, :], gdf[:, :], tmp1[:, :]))
        vstep(ve.tensor_copy(gdf[:, :], tmp2[:, :]))
    vstep(ve.tensor_scalar(tmp1[:, :], keep[:, :], float(-SENT), None, ALU.mult))
    vstep(ve.tensor_scalar(tmp2[:, :], tmp1[:, :], float(SENT), None, ALU.add))
    vstep(ve.tensor_add(tmp1[:, :], gdf[:, :], tmp2[:, :]))
    vstep(ve.tensor_copy(gdU[:, :], tmp1[:, :]))
    vstep(ve.memset(crow[0:1, 0, 0:8], 0.0))
    for j_ in range(8):
        vstep(ve.tensor_scalar(crow[0:1, 1, 0:8], nrow[0:1, 0:8], float(j_ * CHUNK), None, ALU.is_gt))
        vstep(ve.tensor_add(crow[0:1, 2, 0:8], crow[0:1, 0, 0:8], crow[0:1, 1, 0:8]))
        vstep(ve.tensor_copy(crow[0:1, 0, 0:8], crow[0:1, 2, 0:8]))
    vstep(ve.tensor_copy(kcu[0:1, :], crow[0:1, 0, 0:8]))

    if dbg == "route":
        g.wait_ge(sV.h, sV.n)
        sO.inc(g.dma_start(out=dbg_outs["d_rt"][:, 0:4096], in_=wef), 16)
        sO.inc(g.dma_start(out=dbg_outs["d_rt"][:, 4096:4160], in_=w12o[:, :]), 16)
        sO.inc(g.dma_start(out=dbg_outs["d_rt"][:, 4160:4224], in_=e12o[:, :]), 16)
        sO.inc(g.dma_start(out=dbg_outs["d_rt"][:, 4224:4288], in_=keep[:, :]), 16)
        sO.inc(g.dma_start(out=dbg_outs["d_rt"][:, 4288:4352], in_=kwsb[:, :]), 16)
        sO.inc(g.dma_start(out=dbg_outs["d_rt"][:, 4352:4416], in_=gdU[:, :].bitcast(F32)), 16)
        sO.inc(g.dma_start(out=dbg_outs["d_rt"][:, 4416:4480], in_=gacc[:, 0, :]), 16)
        sO.inc(g.dma_start(out=dbg_outs["d_rt"][:, 4480:4544], in_=gacc[:, 1, :]), 16)
        sO.inc(g.dma_start(out=dbg_outs["d_rt"][:, 4544:4608], in_=idxo[:, :]), 16)
        sO.inc(g.dma_start(out=dbg_outs["d_lgg"][:, :], in_=lgf), 16)
        for i_, rowidx in enumerate((TH, IHI, GG, TT_, QQ)):
            sO.inc(g.dma_start(out=dbg_outs["d_row"][i_:i_ + 1, 0:8], in_=R(rowidx)), 16)
        sO.inc(g.dma_start(out=dbg_outs["d_row"][5:6, 0:64], in_=nrow[:, :]), 16)
        sO.inc(g.dma_start(out=dbg_outs["d_row"][6:7, 0:8], in_=kcu[0:1, :].bitcast(F32)), 16)
        g.wait_ge(sO.h, sO.n)
        rt.close()
        return nc, dbg_outs

    g.wait_ge(sV.h, sV.n)
    sy.wait_ge(sV.h, sV.n)
    rt.close()
    sc_stk = ExitStack()
    scL = sc_stk.enter_context(nc.sbuf_tensor("scL", [128, 32, 1024], BF16))
    sH.inc(sy.dma_start(out=scL[:, :, :],
                        in_=ln2_d[ds(r_ox2, 4096), :].rearrange("(gl p) d -> p gl d", p=128)), 16)
    sG.inc(g.dma_start(out=x2o_d[:, :], in_=x2_d[ds(r_ox2, 4096), :]), 16)
    g.wait_ge(sH.h, sH.n)
    g.wait_ge(sG.h, sG.n)
    for c in range(64):
        gl = c % 32
        sG.inc(g.indirect_dma_start(out=bkt_d[:, :],
                                    out_offset=IndirectOffsetOnAxis(gdU[:, c:c + 1], 0),
                                    in_=scL[:, gl, :],
                                    in_offset=None), 16)
    g.wait_ge(sG.h, sG.n)
    sc_stk.close()
    rt.close()
    nc.all_engine_barrier()

    moe = ExitStack()

    def sbm(name, shape, dt):
        return moe.enter_context(nc.sbuf_tensor(name, shape, dt))

    hT = sbm("hT", [128, 8, CHUNK], BF16)
    actb = sbm("actb", [128, 32, CHUNK], BF16)
    wAB = sbm("wAB", [128, 3, 2, 8, 128], BF16)
    wGB = sbm("wGB", [128, 3, 2, 8, 128], BF16)
    w2B = sbm("w2B", [128, 2, 16, 512], BF16)
    siS = sbm("siS", [128, 2, 512], BF16)
    outT = sbm("outT", [128, 8, 512], BF16)
    tokS = sbm("tokS", [128, 4, 1024], BF16)

    from concourse.expressions import RuntimeValue
    moe_loop_regs = nc.alloc_registers("moeloop", engines=mybir.ALL_ENGINES)
    moe_ke_regs = nc.alloc_registers("moeke", engines=mybir.ALL_ENGINES)
    rbase_reg = g.alloc_register("moerbase")
    for e_c in range(E):
        for _eng in (g, sy, pe, ve, ac):
            _eng.reg_load(moe_ke_regs[_eng.engine], kcu[0:1, e_c:e_c + 1])
        ke_v = RuntimeValue(moe_ke_regs, min_val=0, max_val=8)
        nc.regs_mov(moe_loop_regs, 0)
        lstart, lend = f"moe{e_c}_loop", f"moe{e_c}_end"
        nc.br_eq(moe_ke_regs, 0, on_true=lend, on_false=lstart, engines=mybir.ALL_ENGINES)
        with nc.body(lstart, valid_engines=mybir.ALL_ENGINES):
            g.reg_alu(rbase_reg, moe_loop_regs[g.engine], CHUNK, ALU.mult)
            g.reg_alu(rbase_reg, rbase_reg, e_c * BUCKET, ALU.add)
            rbase = RuntimeValue(rbase_reg, min_val=e_c * BUCKET, max_val=e_c * BUCKET + (8 - 1) * CHUNK)
            def wpair(buf, dp):
                for dd in range(2):
                    d_ = dp * 2 + dd
                    g.dma_start(out=wAB[:, buf, dd, :, :],
                                in_=w1t[e_c * D:(e_c + 1) * D, d_ * 128:(d_ + 1) * 128].rearrange("(k p) f -> p k f", p=128)
                                ).then_inc(ms_wt.sem(), 16)
                    g.dma_start(out=wGB[:, buf, dd, :, :],
                                in_=wgt[e_c * D:(e_c + 1) * D, d_ * 128:(d_ + 1) * 128].rearrange("(k p) f -> p k f", p=128)
                                ).then_inc(ms_wt.sem(), 16)

            g.dma_start(out=chk_d[:, :], in_=bkt_d[ds(rbase, CHUNK), :]).then_inc(ms_mg.sem(), 16)
            wpair(0, 0)
            wpair(1, 1)
            ms_mg.wait_inc(16)
            ms_wt.wait_inc(64)
            nc.all_engine_barrier()
            for k in range(8):
                sy.dma_start(out=hT[:, k, :], in_=chk_d[:, k * 128:(k + 1) * 128],
                             transpose=True).then_inc(ms_mv.sem(), 16)
            ms_mv.wait_inc(16 * 8)
            nc.all_engine_barrier()

            def aslot(s):
                return psA[:, (s % 2) * 1024:(s % 2) * 1024 + 1024]

            def acons(s):
                sb = aslot(s)
                ac.activation(siS[:, s % 2, :], sb[:, 512:1024],
                              AF.Silu).then_inc(ms_mv2.sem(), 1)
                ms_mv2.wait_inc(1)
                ve.tensor_tensor(actb[:, s, :], sb[:, 0:512], siS[:, s % 2, :], ALU.mult)

            for s in range(32):
                if s >= 1:
                    acons(s - 1)
                sb = aslot(s)
                for k in range(8):
                    MM(sb[:, 0:512], wAB[:, (s // 2) % 3, s % 2, k, :], hT[:, k, :],
                       start=(k == 0), stop=(k == 7))
                    MM(sb[:, 512:1024], wGB[:, (s // 2) % 3, s % 2, k, :], hT[:, k, :],
                       start=(k == 0), stop=(k == 7))
                if s % 2 == 0 and s // 2 + 2 < 16:
                    wpair((s // 2 + 2) % 3, s // 2 + 2)
                if s % 2 == 1 and (s + 1) // 2 < 16:
                    ms_wt.wait_inc(64)
                nc.all_engine_barrier()
            acons(31)
            for half in range(2):
                for part in range(2):
                    g.dma_start(out=w2B[:, part, :, :],
                                in_=w2t[e_c * DFF + part * 2048:e_c * DFF + (part + 1) * 2048,
                                        half * 512:(half + 1) * 512].rearrange("(k p) f -> p k f", p=128)
                                ).then_inc(ms_mg.sem(), 16)
                ms_mg.wait_inc(16)
                nc.all_engine_barrier()
                for part in range(2):
                    if part == 1:
                        ms_mg.wait_inc(16)
                        nc.all_engine_barrier()
                    for dl in range(16):
                        d_ = part * 16 + dl
                        for cc in range(4):
                            mmb = MM(psA[:, cc * 512:(cc + 1) * 512], w2B[:, part, dl, cc * 128:(cc + 1) * 128],
                                     actb[:, d_, :], start=(d_ == 0), stop=(d_ == 31))
                            if d_ == 31 and cc == 3:
                                mmb.then_inc(ms_mv.sem(), 1)
                ms_mv.wait_inc(1)
                nc.all_engine_barrier()
                for cc in range(4):
                    evb = ac.activation(outT[:, half * 4 + cc, :], psA[:, cc * 512:(cc + 1) * 512],
                                        AF.Identity)
                    if cc == 3:
                        evb.then_inc(ms_mv2.sem(), 1)
                ms_mv2.wait_inc(1)
                nc.all_engine_barrier()
            for sb_ in range(4):
                for c8 in range(8):
                    tr = pe.transpose(psT[:, c8 * 128:(c8 + 1) * 128],
                                      outT[:, c8, sb_ * 128:(sb_ + 1) * 128], c_ident[:, :])
                    if c8 == 7:
                        tr.then_inc(ms_mv.sem(), 1)
                ms_mv.wait_inc(1)
                for c8 in range(8):
                    ve.tensor_copy(tokS[:, sb_, c8 * 128:(c8 + 1) * 128],
                                   psT[:, c8 * 128:(c8 + 1) * 128])
                nc.all_engine_barrier()
            g.dma_start(out=ob_d[ds(rbase, CHUNK), :].rearrange("(bl p) dd -> p bl dd", p=128),
                        in_=tokS[:, :, :]).then_inc(ms_mg.sem(), 16)
            ms_mg.wait_inc(16)
            nc.all_engine_barrier()
            nc.regs_alu(moe_loop_regs, moe_loop_regs, 1, op=ALU.add)
            nc.br_lt(moe_loop_regs, ke_v, on_true=lstart, on_false=lend, engines=mybir.ALL_ENGINES)
        nc.switch_bb(lend)
    moe.close()
    nc.all_engine_barrier()

    if dbg == "moe":
        for e_ in range(8):
            sO.inc(g.dma_start(out=dbg_outs["d_ob"][e_ * 1536:(e_ + 1) * 1536, :],
                               in_=ob_d[e_ * BUCKET:e_ * BUCKET + 1536, :]), 16)
            sO.inc(g.dma_start(out=dbg_outs["d_bk"][e_ * 1536:(e_ + 1) * 1536, :],
                               in_=bkt_d[e_ * BUCKET:e_ * BUCKET + 1536, :]), 16)
            g.wait_ge(sO.h, sO.n)
        sO.inc(g.dma_start(out=dbg_outs["d_lgg"][:, :], in_=shLG[0:TALL, :]), 16)
        g.wait_ge(sO.h, sO.n)
        return nc, dbg_outs

    fin_ = ExitStack()
    gb1 = fin_.enter_context(nc.sbuf_tensor("gb1", [128, 2, 1024], BF16))
    gb2 = fin_.enter_context(nc.sbuf_tensor("gb2", [128, 2, 1024], BF16))
    xf = fin_.enter_context(nc.sbuf_tensor("xf", [128, 2, 1024], F32))
    of1 = fin_.enter_context(nc.sbuf_tensor("of1", [128, 1024], F32))
    of2 = fin_.enter_context(nc.sbuf_tensor("of2", [128, 2, 1024], F32))

    for ct in range(32):
        cb = ct % 2
        if ct >= 2:
            g.wait_ge(sV.h, sV.n)
        sG.inc(g.indirect_dma_start(out=gb1[:, cb, :], out_offset=None,
                                    in_=ob_d[:, :],
                                    in_offset=IndirectOffsetOnAxis(gdU[:, ct:ct + 1], 0)), 16)
        sG.inc(g.indirect_dma_start(out=gb2[:, cb, :], out_offset=None,
                                    in_=ob_d[:, :],
                                    in_offset=IndirectOffsetOnAxis(gdU[:, 32 + ct:33 + ct], 0)), 16)
        if ct >= 2:
            sy.wait_ge(sV.h, sV.n)
        sH.inc(sy.dma_start(out=xf[:, cb, :],
                            in_=x2o_d[ct * 128:(ct + 1) * 128, :]), 16)
        ve.wait_ge(sG.h, sG.n)
        ve.wait_ge(sH.h, sH.n)
        ve.wait_ge(sV.h, sV.n)
        sV.inc(ve.scalar_tensor_tensor(of1[:, :], gb1[:, cb, :], kwsb[:, ct:ct + 1],
                                       xf[:, cb, :], ALU.mult, ALU.add), 1)
        ve.wait_ge(sV.h, sV.n)
        if ct >= 2:
            ve.wait_ge(sO.h, sO.n - 16)
        sV.inc(ve.scalar_tensor_tensor(of2[:, cb, :], gb2[:, cb, :], kwsb[:, 32 + ct:33 + ct],
                                       of1[:, :], ALU.mult, ALU.add), 1)
        g.wait_ge(sV.h, sV.n)
        sO.inc(g.dma_start(out=out[ct * 128:(ct + 1) * 128, :], in_=of2[:, cb, :]), 16)
    g.wait_ge(sO.h, sO.n)
    fin_.close()
    return nc, dbg_outs


def _host_prep(inputs):
    x = np.asarray(inputs["L0_attn_x"], dtype=np.float32)
    wqkv = np.ascontiguousarray(np.asarray(inputs["L0_attn_W_qkv"], dtype=np.float32))
    wo = np.ascontiguousarray(np.asarray(inputs["L0_attn_W_o"], dtype=np.float32))
    rw = np.ascontiguousarray(np.asarray(inputs["L0_ffn_router_weight"], dtype=np.float32))
    w1 = np.asarray(inputs["L0_ffn_ff_pre_act_weight"], dtype=np.float32)
    wg = np.asarray(inputs["L0_ffn_gate_weight"], dtype=np.float32)
    w2 = np.asarray(inputs["L0_ffn_ff_post_act_weight"], dtype=np.float32)
    w1t = np.ascontiguousarray(w1.transpose(0, 2, 1)).reshape(E * D, DFF)
    wgt = np.ascontiguousarray(wg.transpose(0, 2, 1)).reshape(E * D, DFF)
    w2t = np.ascontiguousarray(w2.transpose(0, 2, 1)).reshape(E * DFF, D)
    in_maps = []
    for c in range(NCORE):
        kk = c // 2
        if c % 2 == 0:
            xh = np.ascontiguousarray(x[0:8].reshape(TOK, D))
            xbase = 0
            obase = kk * 4096
        else:
            xh = np.ascontiguousarray(x[8:16].reshape(TOK, D))
            xbase = TOK
            obase = TOK + kk * 4096
        prmv = np.array([[xbase, obase - xbase, c % 2, obase // 128]], dtype=np.uint32)
        prmfv = np.array([[float(obase // 128), 256.0 * (obase // 128), 0.0, 0.0]], dtype=np.float32)
        in_maps.append({"xh": xh, "wqkv": wqkv, "wo": wo, "rw": rw,
                       "w1t": w1t, "wgt": wgt, "w2t": w2t, "prm": prmv, "prmf": prmfv})
    return in_maps


def kernel(**inputs):
    in_maps = _host_prep(inputs)
    nc, _ = build(dbg=DEBUG_STAGE)
    lower_extended_insts(nc)
    res = run_bass_kernel_spmd(nc, in_maps, core_ids=list(range(NCORE)))
    if DEBUG_STAGE is not None:
        return res
    outf = np.zeros((B, T, D), dtype=np.float32)
    for c in range(NCORE):
        o = np.asarray(res.results[c]["out"]).reshape(2, T, D)
        kk = c // 2
        if c % 2 == 0:
            outf[2 * kk] = o[0]
            outf[2 * kk + 1] = o[1]
        else:
            outf[8 + 2 * kk] = o[0]
            outf[8 + 2 * kk + 1] = o[1]
    return outf


# version 08
# speedup vs baseline: 1.0639x; 1.0639x over previous
import numpy as np
import ml_dtypes
from contextlib import ExitStack

import concourse.bass as bass
import concourse.mybir as mybir
from concourse.bass import ds, IndirectOffsetOnAxis
from concourse.bass_utils import run_bass_kernel_spmd
from concourse.library_overlay import lower_extended_insts

F32 = mybir.dt.float32
BF16 = mybir.dt.bfloat16
U32 = mybir.dt.uint32
AF = mybir.ActivationFunctionType
ALU = mybir.AluOpType
AX = mybir.AxisListType

NCORE = 8
B, T, D, NH, DH = 16, 2048, 1024, 16, 64
NB = 8
TOK = NB * T
TALL = B * T
E, DFF = 8, 4096
CAP = 10240.0
EPS = 1.1920929e-07
BUCKET = 4096
SENT = E * BUCKET
CHUNK = 512
VAL_ROUNDS = 42
IDX_ROUNDS = 17
MAGIC = 0x5A17CA3E

DEBUG_STAGE = None


class CS:
    def __init__(self, nc, name):
        self.h = nc.alloc_semaphore(name=name)
        self.n = 0

    def inc(self, instr, k=1):
        instr.then_inc(self.h, k)
        self.n += k
        return instr


def _consts():
    c = {}
    half = DH // 2
    freqs = np.arange(half, dtype=np.float32) / half
    theta = (1.0 / (10000.0 ** freqs)).astype(np.float32)
    t = np.arange(T, dtype=np.float32)
    ang = t[:, None] * theta[None, :]
    c["ropec"] = np.cos(ang).astype(np.float32)
    c["ropes"] = np.sin(ang).astype(np.float32)
    c["ident"] = np.eye(128, dtype=np.float32)
    ps = np.arange(128)[:, None]
    ft = np.arange(512)[None, :]
    m = np.zeros((128, 4, 512), dtype=np.float32)
    for j in range(4):
        m[:, j, :] = (ps <= ft - 128 * j).astype(np.float32)
    c["masks"] = m.reshape(128, 2048)
    c["onesr"] = np.ones((1, 128), dtype=np.float32)
    c["tri"] = (np.arange(128)[:, None] < np.arange(128)[None, :]).astype(np.float32)
    p = np.arange(128, dtype=np.float32)[:, None]
    gg = np.arange(256, dtype=np.float32)[None, :]
    tRT = (gg * 128.0 + p).astype(np.float32)
    c["tRT"] = tRT
    c["io8"] = np.broadcast_to(np.arange(8, dtype=np.float32), (128, 8)).copy()
    c["io8m"] = (c["io8"] - 999.0).copy()
    idc = np.concatenate([2.0 * tRT, 2.0 * tRT + 1.0], axis=1).astype(np.float32)
    c["idc"] = idc
    c["idcm"] = (idc - 131072.0).copy()
    kp = np.zeros((128, 64), dtype=np.float32)
    kp[:, 32:] = 1.0
    c["kpat"] = kp
    pidx = np.zeros((128, 64), dtype=np.float32)
    pp_ = np.arange(128)[:, None]
    gl_ = np.arange(32)[None, :]
    pidx[:, 0:32] = 2.0 * (gl_ * 128 + pp_) + 0.0
    pidx[:, 32:64] = 2.0 * (gl_ * 128 + pp_) + 1.0
    c["idxpat"] = pidx
    return c


def build(dbg=None):
    nc = bass.Bass("TRN2", target_bir_lowering=False, debug=False, num_devices=8,
                   monotonic_sem_count=5)
    CST = _consts()

    xh = nc.dram_tensor("xh", [TOK, D], F32, kind="ExternalInput").ap()
    wqkv = nc.dram_tensor("wqkv", [D, 3 * D], F32, kind="ExternalInput").ap()
    wo = nc.dram_tensor("wo", [D, D], F32, kind="ExternalInput").ap()
    rw = nc.dram_tensor("rw", [E, D], F32, kind="ExternalInput").ap()
    w1t = nc.dram_tensor("w1t", [E * D, DFF], F32, kind="ExternalInput").ap()
    wgt = nc.dram_tensor("wgt", [E * D, DFF], F32, kind="ExternalInput").ap()
    w2t = nc.dram_tensor("w2t", [E * DFF, D], F32, kind="ExternalInput").ap()
    prm = nc.dram_tensor("prm", [1, 4], U32, kind="ExternalInput").ap()
    prmf = nc.dram_tensor("prmf", [1, 4], F32, kind="ExternalInput").ap()
    out = nc.dram_tensor("out", [4096, D], F32, kind="ExternalOutput").ap()

    xbf_d = nc.dram_tensor("xbf_d", [T, D], BF16).ap()
    qk_d = nc.dram_tensor("qk_d", [T, 2 * D], BF16).ap()
    yt_d = nc.dram_tensor("yt_d", [D, T], BF16).ap()
    mr_d = nc.dram_tensor("mr_d", [1, 4096], BF16).ap()
    rt_d = nc.dram_tensor("rt_d", [128, 256, 4], F32).ap()
    x2o_d = nc.dram_tensor("x2o_d", [4096, D], F32).ap()
    chk_d = nc.dram_tensor("chk_d", [CHUNK, D], BF16).ap()
    x2_d = nc.dram_tensor("x2_d", [TOK, D], F32).ap()
    ln2_d = nc.dram_tensor("ln2_d", [TOK, D], BF16).ap()
    lg_d = nc.dram_tensor("lg_d", [TOK, E], F32).ap()
    bkt_d = nc.dram_tensor("bkt_d", [SENT + 1, D], BF16).ap()
    ob_d = nc.dram_tensor("ob_d", [SENT + 1, D], BF16).ap()
    shLG = nc.dram_tensor("shLG", [TALL, E], F32, addr_space="Shared").ap()
    shFL = nc.dram_tensor("shFL", [2, 64], U32, addr_space="Shared").ap()

    dbg_outs = {}
    if dbg == "attn":
        dbg_outs["d_x2"] = nc.dram_tensor("d_x2", [TOK, D], F32, kind="ExternalOutput").ap()
        dbg_outs["d_lg"] = nc.dram_tensor("d_lg", [TOK, E], F32, kind="ExternalOutput").ap()
        dbg_outs["d_qk"] = nc.dram_tensor("d_qk", [T, 2 * D], F32, kind="ExternalOutput").ap()
        dbg_outs["d_yt"] = nc.dram_tensor("d_yt", [D, T], F32, kind="ExternalOutput").ap()
        dbg_outs["d_xbf"] = nc.dram_tensor("d_xbf", [T, D], F32, kind="ExternalOutput").ap()
        dbg_outs["d_vt"] = nc.dram_tensor("d_vt", [128, 16 * 1040], F32, kind="ExternalOutput").ap()
    if dbg == "route":
        dbg_outs["d_rt"] = nc.dram_tensor("d_rt", [128, 4608], F32, kind="ExternalOutput").ap()
        dbg_outs["d_row"] = nc.dram_tensor("d_row", [16, 64], F32, kind="ExternalOutput").ap()
        dbg_outs["d_lgg"] = nc.dram_tensor("d_lgg", [128, 2048], F32, kind="ExternalOutput").ap()
    if dbg == "moe":
        dbg_outs["d_ob"] = nc.dram_tensor("d_ob", [8 * 1536, D], BF16, kind="ExternalOutput").ap()
        dbg_outs["d_bk"] = nc.dram_tensor("d_bk", [8 * 1536, D], BF16, kind="ExternalOutput").ap()
        dbg_outs["d_lgg"] = nc.dram_tensor("d_lgg", [TALL, E], F32, kind="ExternalOutput").ap()

    def inl(name, arr, dt=F32):
        a = np.ascontiguousarray(arr.astype(ml_dtypes.bfloat16) if dt == BF16 else
                                 arr.astype(np.float32))
        return nc.inline_tensor(a, name=name).ap()

    ropec_c = inl("ropec", CST["ropec"])
    ropes_c = inl("ropes", CST["ropes"])
    ident_c = inl("identb", CST["ident"], BF16)
    identf_c = inl("identf", CST["ident"])
    masks_c = inl("masks", CST["masks"], BF16)
    onesr_c = inl("onesrb", CST["onesr"], BF16)
    onesrf_c = inl("onesrf", CST["onesr"])
    tri_c = inl("trib", CST["tri"], BF16)
    tRT_c = inl("tRT", CST["tRT"])
    io8_c = inl("io8", CST["io8"])
    io8m_c = inl("io8m", CST["io8m"])
    idc_c = inl("idc", CST["idc"])
    idcm_c = inl("idcm", CST["idcm"])
    kpat_c = inl("kpat", CST["kpat"])
    idxpat_c = inl("idxpat", CST["idxpat"])

    g = nc.gpsimd
    sy = nc.sync
    pe = nc.tensor
    ve = nc.vector
    ac = nc.scalar
    MM = pe.matmul

    sG = CS(nc, "sG")
    sH = CS(nc, "sH")
    sP = CS(nc, "sP")
    sV = CS(nc, "sV")
    sA = CS(nc, "sA")
    sO = CS(nc, "sO")
    ms_poll = nc.monotonic_semaphore(0, engine=nc.gpsimd)
    ms_mg = nc.monotonic_semaphore(1, engine=nc.gpsimd)
    ms_mv = nc.monotonic_semaphore(2, engine=nc.vector)
    ms_mv2 = nc.monotonic_semaphore(3, engine=nc.vector)
    ms_wt = nc.monotonic_semaphore(4, engine=nc.gpsimd)

    psA = nc.alloc_psum_tensor("psA", [128, 2048], F32)
    psW = nc.alloc_psum_tensor("psW", [128, 1024], F32)
    psV = nc.alloc_psum_tensor("psV", [128, 512], F32)
    psT = nc.alloc_psum_tensor("psT", [128, 1024], BF16)

    sbl = nc.alloc_sbuf_tensor
    c_ropec = sbl("c_ropec", [128, 16, 32], F32)
    c_ropes = sbl("c_ropes", [128, 16, 32], F32)
    c_ident = sbl("c_ident", [128, 128], BF16)
    c_identf = sbl("c_identf", [128, 128], F32)
    c_masks = sbl("c_masks", [128, 2048], BF16)
    c_onesr = sbl("c_onesr", [1, 128], BF16)
    c_onesrf = sbl("c_onesrf", [1, 128], F32)
    c_onescf = sbl("c_onescf", [128, 1], F32)
    c_tri = sbl("c_tri", [128, 128], BF16)
    c_tRT = sbl("c_tRT", [128, 256], F32)
    c_io8 = sbl("c_io8", [128, 8], F32)
    c_io8m = sbl("c_io8m", [128, 8], F32)
    c_idc = sbl("c_idc", [128, 512], F32)
    c_idcm = sbl("c_idcm", [128, 512], F32)
    c_kpat = sbl("c_kpat", [128, 64], F32)
    c_idxp = sbl("c_idxp", [128, 64], F32)
    prmft = sbl("prmft", [1, 4], F32)
    onescol = sbl("onescol", [128, 1], BF16)
    c_eps = sbl("c_eps", [128, 1], F32)
    zrow = sbl("zrow", [1, 1024], BF16)
    prmt = sbl("prmt", [1, 4], U32)
    flagw = sbl("flagw", [1, 1], U32)
    flrd = sbl("flrd", [1, 1], U32)
    rwT = sbl("rwT", [128, 8, 8], BF16)
    rwbf = sbl("rwbf", [8, 1024], BF16)
    kwsb = sbl("kwsb", [128, 64], F32)
    gdU = sbl("gdU", [128, 64], U32)
    kcu = sbl("kcu", [1, 8], U32)

    sG.inc(g.dma_start(out=prmt[:, :], in_=prm[:, :]), 16)
    for apdst, apsrc in [(c_ropec[:, :, :], ropec_c.rearrange("(k p) f -> p k f", p=128)),
                         (c_ropes[:, :, :], ropes_c.rearrange("(k p) f -> p k f", p=128)),
                         (c_ident[:, :], ident_c[:, :]), (c_identf[:, :], identf_c[:, :]),
                         (c_masks[:, :], masks_c[:, :]), (c_onesr[:, :], onesr_c[:, :]),
                         (c_onesrf[:, :], onesrf_c[:, :]), (c_tri[:, :], tri_c[:, :]),
                         (c_tRT[:, :], tRT_c[:, :]), (c_io8[:, :], io8_c[:, :]),
                         (c_io8m[:, :], io8m_c[:, :]), (c_idc[:, :], idc_c[:, :]),
                         (c_idcm[:, :], idcm_c[:, :]), (c_kpat[:, :], kpat_c[:, :]),
                         (c_idxp[:, :], idxpat_c[:, :]), (prmft[:, :], prmf[:, :])]:
        sH.inc(sy.dma_start(out=apdst, in_=apsrc), 16)
    sG.inc(g.dma_start(out=rwbf[:, :], in_=rw[:, :]), 16)
    g.wait_ge(sG.h, sG.n)
    ve.wait_ge(sG.h, sG.n)
    ve.wait_ge(sH.h, sH.n)
    sV.inc(ve.memset(onescol[:, :], 1.0), 1)
    sV.inc(ve.memset(c_onescf[:, :], 1.0), 1)
    sV.inc(ve.memset(zrow[:, :], 0.0), 1)
    sV.inc(ve.memset(flagw[:, :], MAGIC), 1)
    sV.inc(ve.memset(c_eps[:, :], 1e-5), 1)
    nc.const_aps.register(F32, 1e-5, c_eps[:, 0:1])
    ve.wait_ge(sV.h, sV.n)
    sG.inc(g.dma_start(out=ob_d[SENT:SENT + 1, :], in_=zrow[:, :]), 16)
    pe.wait_ge(sV.h, sV.n)
    for k in range(8):
        sP.inc(pe.transpose(psT[:, 0:8], rwbf[0:8, k * 128:(k + 1) * 128], c_ident[0:8, 0:8]), 1)
        ve.wait_ge(sP.h, sP.n)
        sV.inc(ve.tensor_copy(rwT[:, k, :], psT[:, 0:8]), 1)
        pe.wait_ge(sV.h, sV.n)
    g.wait_ge(sG.h, sG.n)
    r_xbase = nc.values_load(prmt[0:1, 0:1], engines=mybir.ALL_ENGINES, min_val=0, max_val=TALL - TOK)
    r_par = nc.values_load(prmt[0:1, 2:3], engines=mybir.ALL_ENGINES, min_val=0, max_val=1)
    nc.all_engine_barrier()

    att = ExitStack()

    def sba(name, shape, dt):
        return att.enter_context(nc.sbuf_tensor(name, shape, dt))

    woT = sba("woT", [128, 8, 2, 512], BF16)
    xa = sba("xa", [128, 2, 1024], F32)
    xbfb = sba("xbfb", [128, 2, 1024], BF16)
    stat = sba("stat", [128, 8, 16], F32)
    statT = sba("statT", [32, 128], F32)
    mrow = sba("mrow", [1, 2048], BF16)
    rrow = sba("rrow", [1, 2048], BF16)
    scr = sba("scr", [128, 2048], BF16)
    scf = sba("scf", [128, 1024], F32)
    regW = sba("regW", [128, 2, 16, 512], BF16)
    wq_done = {}

    def issue_qkv_pass(pg):
        pss_ = pg % 3
        for nn in range(2):
            for k in range(8):
                n = pss_ * 2 + nn
                sG.inc(g.dma_start(out=regW[:, pg % 2, nn * 8 + k, :],
                                   in_=wqkv[k * 128:(k + 1) * 128, n * 512:(n + 1) * 512]), 16)
        wq_done[pg] = sG.n
    lnT = sba("lnT", [128, 8, 2048], BF16)
    vt = sba("vt", [128, 16, 1040], BF16)
    qkstg = sba("qkstg", [128, 2, 512], BF16)
    qkro = sba("qkro", [128, 2, 1024], BF16)
    qkn = sba("qkn", [128, 2, 64], F32)
    bndrow = sba("bndrow", [1, 64], F32)
    nbias = sba("nbias", [128, 16], F32)
    PTb = sba("PTb", [128, 2, 2048], BF16)
    PTm = sba("PTm", [128, 2048], BF16)
    lbc = sba("lbc", [64, 512], F32)
    linv = sba("linv", [1, 512], F32)
    yTq = sba("yTq", [128, 2, 2048], BF16)
    x2t = sba("x2t", [128, 2, 1024], F32)
    ln2t = sba("ln2t", [128, 2, 1024], BF16)
    lgst = sba("lgst", [128, 16, 8], F32)

    for k in range(8):
        sG.inc(g.dma_start(out=woT[:, k, :, :],
                           in_=wo[k * 128:(k + 1) * 128, :].rearrange("p (n f) -> p n f", n=2)), 16)
    issue_qkv_pass(0)

    for b in range(NB):
        xrow = b * T
        for k in range(16):
            kb = k % 2
            if k >= 2:
                sy.wait_ge(sA.h, sA.n - 2)
            sH.inc(sy.dma_start(out=xa[:, kb, :], in_=xh[xrow + k * 128:xrow + (k + 1) * 128, :]), 16)
            ac.wait_ge(sH.h, sH.n)
            if k >= 2:
                ac.wait_ge(sG.h, sG.n)
            sA.inc(ac.activation(xbfb[:, kb, :], xa[:, kb, :], AF.Identity,
                                 accum_out=stat[:, 0, k:k + 1]), 1)
            sA.inc(ac.activation(scr[:, 0:1024], xa[:, kb, :], AF.Square,
                                 accum_out=stat[:, 1, k:k + 1]), 1)
            g.wait_ge(sA.h, sA.n - 1)
            sG.inc(g.dma_start(out=xbf_d[k * 128:(k + 1) * 128, :], in_=xbfb[:, kb, :]), 16)
        ve.wait_ge(sA.h, sA.n)
        sV.inc(ve.tensor_scalar(stat[:, 2, :], stat[:, 0, :], -1.0 / D, None, ALU.mult), 1)
        sV.inc(ve.tensor_scalar(stat[:, 3, :], stat[:, 1, :], 1.0 / D, None, ALU.mult), 1)
        ve.wait_ge(sV.h, sV.n)
        sV.inc(ve.tensor_mul(stat[:, 4, :], stat[:, 2, :], stat[:, 2, :]), 1)
        ve.wait_ge(sV.h, sV.n)
        sV.inc(ve.tensor_sub(stat[:, 5, :], stat[:, 3, :], stat[:, 4, :]), 1)
        ac.wait_ge(sV.h, sV.n)
        sA.inc(ac.activation(stat[:, 6, :], stat[:, 5, :], AF.Ln, bias=1e-5), 1)
        ac.wait_ge(sA.h, sA.n)
        sA.inc(ac.activation(stat[:, 3, :], stat[:, 6, :], AF.Exp, scale=-0.5), 1)
        pe.wait_ge(sA.h, sA.n)
        sP.inc(pe.transpose(psW[0:32, 0:128], stat[:, 2:4, :].rearrange("p a bb -> p (a bb)"),
                            c_identf[:, :]), 1)
        ve.wait_ge(sP.h, sP.n)
        sV.inc(ve.tensor_copy(statT[:, :], psW[0:32, 0:128]), 1)
        g.wait_ge(sV.h, sV.n)
        sG.inc(g.dma_start(out=mr_d.rearrange("a (r p) -> (a r) p", p=128), in_=statT[:, :]), 16)
        g.wait_ge(sG.h, sG.n)
        sG.inc(g.dma_start(out=mrow[:, :], in_=mr_d[0:1, 0:2048]), 16)
        sG.inc(g.dma_start(out=rrow[:, :], in_=mr_d[0:1, 2048:4096]), 16)
        g.wait_ge(sG.h, sG.n)
        nc.all_engine_barrier()

        for q in range(8):
            if q >= 1:
                sy.wait_ge(sV.h, sV.n)
            sH.inc(sy.dma_start(out=scr[:, :], in_=xbf_d[:, q * 128:(q + 1) * 128],
                                transpose=True), 16)
            for h2 in range(2):
                sl = slice(h2 * 1024, (h2 + 1) * 1024)
                pe.wait_ge(sH.h, sH.n)
                pe.wait_ge(sV.h, sV.n)
                for hh in range(2):
                    s2 = slice(h2 * 1024 + hh * 512, h2 * 1024 + (hh + 1) * 512)
                    sP.inc(MM(psW[:, hh * 512:(hh + 1) * 512], c_onesr[0:1, :],
                              mrow[0:1, s2], start=True, stop=True), 1)
                    sP.inc(MM(psA[:, hh * 512:(hh + 1) * 512], c_onesr[0:1, :],
                              rrow[0:1, s2], start=True, stop=True), 1)
                ve.wait_ge(sP.h, sP.n)
                ve.wait_ge(sH.h, sH.n)
                sV.inc(ve.tensor_tensor(scf[:, :], scr[:, sl], psW[:, :], ALU.add), 1)
                ve.wait_ge(sV.h, sV.n)
                sV.inc(ve.tensor_tensor(lnT[:, q, sl], scf[:, :], psA[:, 0:1024], ALU.mult), 1)
                ve.wait_ge(sV.h, sV.n)
        nc.all_engine_barrier()

        sV.inc(ve.memset(qkn[:, :, :], 0.0), 1)
        sV.inc(ve.memset(vt.ap().rearrange("p t (h f) -> p t h f", f=65)[:, :, :, 64:65], 1.0), 1)
        ve.wait_ge(sV.h, sV.n)
        nc.all_engine_barrier()
        for pss in range(3):
            pg = b * 3 + pss
            if pg + 1 < NB * 3:
                issue_qkv_pass(pg + 1)
            pe.wait_ge(sG.h, wq_done[pg])
            for t_ in range(16):
                qa = t_ % 2
                for nn in range(2):
                    m = t_ * 2 + nn
                    j = m % 4
                    if m >= 4:
                        pe.wait_ge(sA.h, sA.n - 3)
                    if nn == 0 and t_ >= 2 and pss < 2:
                        pe.wait_ge(sG.h, sG.n)
                    for k in range(8):
                        sP.inc(MM(psA[:, j * 512:(j + 1) * 512], lnT[:, k, t_ * 128:(t_ + 1) * 128],
                                  regW[:, pg % 2, nn * 8 + k, :], start=(k == 0), stop=(k == 7)), 1)
                    ac.wait_ge(sP.h, sP.n)
                    if pss < 2:
                        coff = nn * 512
                        if m >= 2:
                            ac.wait_ge(sV.h, sV.n - 2)
                        sA.inc(ac.activation(qkstg[:, nn, :], psA[:, j * 512:(j + 1) * 512],
                                             AF.Identity), 1)
                        ve.wait_ge(sA.h, sA.n)
                        s8 = qkstg[:, nn, :].rearrange("p (h f) -> p h f", h=8)
                        dsl = qkro[:, qa, coff:coff + 512].rearrange("p (h f) -> p h f", h=8)
                        cosv = c_ropec[:, t_, :].unsqueeze(1).to_broadcast([128, 8, 32])
                        sinv = c_ropes[:, t_, :].unsqueeze(1).to_broadcast([128, 8, 32])
                        t8 = scr[:, 0:1024].rearrange("p (a h f) -> p a h f", a=4, h=8)
                        sV.inc(ve.tensor_tensor(t8[:, 0], s8[:, :, 0:32], cosv, ALU.mult), 1)
                        sV.inc(ve.tensor_tensor(t8[:, 1], s8[:, :, 32:64], sinv, ALU.mult), 1)
                        sV.inc(ve.tensor_tensor(t8[:, 2], s8[:, :, 0:32], sinv, ALU.mult), 1)
                        sV.inc(ve.tensor_tensor(t8[:, 3], s8[:, :, 32:64], cosv, ALU.mult), 1)
                        ve.wait_ge(sV.h, sV.n)
                        sV.inc(ve.tensor_sub(dsl[:, :, 0:32], t8[:, 0], t8[:, 1]), 1)
                        sV.inc(ve.tensor_add(dsl[:, :, 32:64], t8[:, 2], t8[:, 3]), 1)
                    else:
                        hv = nn * 8
                        vdst = vt.ap().rearrange("p t (h f) -> p t h f", f=65)[:, t_, hv:hv + 8, 0:64]
                        sA.inc(ac.activation(vdst, psA[:, j * 512:(j + 1) * 512], AF.Identity), 1)
                if pss < 2:
                    pp, qq = t_ % 2, 1 - t_ % 2
                    co = 0 if pss == 0 else 32
                    ve.wait_ge(sV.h, sV.n)
                    sV.inc(ve.tensor_mul(scr[:, 0:1024], qkro[:, qa, :], qkro[:, qa, :]), 1)
                    ve.wait_ge(sV.h, sV.n)
                    sV.inc(ve.tensor_reduce(stat[:, 0, :], scr[:, 0:1024].rearrange("p (h f) -> p h f", h=16),
                                            AX.X, ALU.add), 1)
                    ve.wait_ge(sV.h, sV.n)
                    sV.inc(ve.tensor_max(qkn[:, qq, co:co + 16], qkn[:, pp, co:co + 16], stat[:, 0, :]), 1)
                    g.wait_ge(sV.h, sV.n)
                    sG.inc(g.dma_start(out=qk_d[t_ * 128:(t_ + 1) * 128, pss * 1024:(pss + 1) * 1024],
                                       in_=qkro[:, qa, :]), 16)
            g.wait_ge(sG.h, sG.n)
            nc.all_engine_barrier()
        pe.wait_ge(sV.h, sV.n)
        sP.inc(pe.transpose(psW[0:64, 0:128], qkn[:, 0, :], c_identf[:, :]), 1)
        ve.wait_ge(sP.h, sP.n)
        sV.inc(ve.tensor_reduce(stat[0:64, 0, 0:1], psW[0:64, 0:128], AX.X, ALU.max), 1)
        pe.wait_ge(sV.h, sV.n)
        sP.inc(pe.transpose(psW[0:1, 0:64], stat[0:64, 0, 0:1], c_identf[0:64, 0:64]), 1)
        ve.wait_ge(sP.h, sP.n)
        sV.inc(ve.tensor_copy(bndrow[0:1, 0:64], psW[0:1, 0:64]), 1)
        ve.wait_ge(sV.h, sV.n)
        sV.inc(ve.tensor_mul(bndrow[0:1, 16:32], bndrow[0:1, 0:16], bndrow[0:1, 32:48]), 1)
        ac.wait_ge(sV.h, sV.n)
        sA.inc(ac.activation(bndrow[0:1, 48:64], bndrow[0:1, 16:32], AF.Sqrt), 1)
        ve.wait_ge(sA.h, sA.n)
        sV.inc(ve.tensor_scalar(bndrow[0:1, 0:16], bndrow[0:1, 48:64], -0.1275, None, ALU.mult), 1)
        pe.wait_ge(sV.h, sV.n)
        sP.inc(MM(psW[:, 0:16], c_onesrf[0:1, :], bndrow[0:1, 0:16], start=True, stop=True), 1)
        ac.wait_ge(sP.h, sP.n)
        sA.inc(ac.activation(nbias[:, :], psW[:, 0:16], AF.Identity), 1)
        g.wait_ge(sG.h, sG.n)
        ve.wait_ge(sA.h, sA.n)
        nc.all_engine_barrier()

        for Q in range(4):
            for cq in range(2):
                sH.inc(sy.dma_start(out=lnT[:, cq, :],
                                    in_=qk_d[:, Q * 256 + cq * 128:Q * 256 + (cq + 1) * 128],
                                    transpose=True), 16)
                sH.inc(sy.dma_start(out=lnT[:, 2 + cq, :],
                                    in_=qk_d[:, 1024 + Q * 256 + cq * 128:1024 + Q * 256 + (cq + 1) * 128],
                                    transpose=True), 16)
            ve.wait_ge(sH.h, sH.n)
            nc.all_engine_barrier()
            ptbf = PTb.ap().rearrange("p a f -> p (a f)")
            st4 = {"pend": None, "mul": sV.n}
            for l in range(4):
                hq = 4 * Q + l
                cq, rb = l // 2, (l % 2) * 64
                for tc in range(4):
                    nh = 2 * (tc + 1)
                    e0 = sA.n
                    score_done, exp_done, pv_done, mask_done = {}, {}, {}, {}

                    def emit_scores(j):
                        pb = j % 2
                        pe.wait_ge(sA.h, e0 if j <= 1 else exp_done[j - 2])
                        for si in range(2):
                            s = 2 * j + si
                            sP.inc(MM(psA[:, pb * 1024 + si * 512:pb * 1024 + (si + 1) * 512],
                                      lnT[rb:rb + 64, 2 + cq, s * 128:(s + 1) * 128],
                                      lnT[rb:rb + 64, cq, tc * 512:(tc + 1) * 512],
                                      start=True, stop=True), 1)
                        score_done[j] = sP.n
                        ac.wait_ge(sP.h, score_done[j])
                        if j >= 4:
                            ac.wait_ge(sP.h, pv_done[j - 4])
                        sA.inc(ac.activation(ptbf[:, (j % 4) * 1024:(j % 4 + 1) * 1024],
                                             psA[:, pb * 1024:pb * 1024 + 1024], AF.Exp,
                                             bias=nbias[:, hq:hq + 1], scale=0.125), 1)
                        exp_done[j] = sA.n
                        if j >= nh - 2:
                            dv = j - (nh - 2)
                            ve.wait_ge(sA.h, exp_done[j])
                            sV.inc(ve.tensor_mul(PTm[:, dv * 1024:(dv + 1) * 1024],
                                                 ptbf[:, (j % 4) * 1024:(j % 4 + 1) * 1024],
                                                 c_masks[:, dv * 1024:(dv + 1) * 1024]), 1)
                            mask_done[j] = sV.n

                    def emit_pv(j):
                        pe.wait_ge(sA.h, exp_done[j])
                        if j in mask_done:
                            pe.wait_ge(sV.h, mask_done[j])
                        if j == 0:
                            pe.wait_ge(sV.h, st4["mul"])
                        for si in range(2):
                            s = 2 * j + si
                            if j in mask_done:
                                dv = j - (nh - 2)
                                rhs = PTm[:, dv * 1024 + si * 512:dv * 1024 + (si + 1) * 512]
                            else:
                                rhs = ptbf[:, (j % 4) * 1024 + si * 512:(j % 4) * 1024 + (si + 1) * 512]
                            sP.inc(MM(psV[0:65, :], vt[:, s, hq * 65:(hq + 1) * 65], rhs,
                                      start=(j == 0 and si == 0), stop=(j == nh - 1 and si == 1)), 1)
                        pv_done[j] = sP.n

                    def flush_tail():
                        p = st4["pend"]
                        if p is None:
                            return
                        pvn, mpr, cq_, rb_, tc_ = p
                        ve.wait_ge(sP.h, pvn)
                        sV.inc(ve.reciprocal(linv[0:1, :], psV[64:65, :]), 1)
                        pe.wait_ge(sV.h, sV.n)
                        sP.inc(MM(psW[0:64, 0:512], c_onesrf[0:1, 0:64], linv[0:1, :],
                                  start=True, stop=True), 1)
                        ac.wait_ge(sP.h, sP.n)
                        ac.wait_ge(sV.h, mpr)
                        sA.inc(ac.activation(lbc[:, :], psW[0:64, 0:512], AF.Identity), 1)
                        ve.wait_ge(sA.h, sA.n)
                        sV.inc(ve.tensor_mul(yTq[rb_:rb_ + 64, cq_, tc_ * 512:(tc_ + 1) * 512],
                                             psV[0:64, :], lbc[:, :]), 1)
                        st4["mul"] = sV.n
                        st4["pend"] = None

                    emit_scores(0)
                    flush_tail()
                    for j in range(1, nh):
                        emit_scores(j)
                        emit_pv(j - 1)
                    emit_pv(nh - 1)
                    st4["pend"] = (sP.n, st4["mul"], cq, rb, tc)
            flush_tail()
            g.wait_ge(sV.h, sV.n)
            for cq in range(2):
                sG.inc(g.dma_start(out=yt_d[Q * 256 + cq * 128:Q * 256 + (cq + 1) * 128, :],
                                   in_=yTq[:, cq, :]), 16)
            g.wait_ge(sG.h, sG.n)
            nc.all_engine_barrier()

        for k in range(8):
            sH.inc(sy.dma_start(out=lnT[:, k, :], in_=yt_d[k * 128:(k + 1) * 128, :]), 16)
        ve.wait_ge(sH.h, sH.n)
        nc.all_engine_barrier()
        g5 = sG.n
        add_done = []
        for t_ in range(16):
            kb = t_ % 2
            tgt = psW[:, 0:1024] if kb == 0 else psA[:, 0:1024]
            if t_ >= 2:
                sy.wait_ge(sV.h, add_done[t_ - 2])
            sH.inc(sy.dma_start(out=xa[:, kb, :], in_=xh[xrow + t_ * 128:xrow + (t_ + 1) * 128, :]), 16)
            if t_ >= 2:
                pe.wait_ge(sV.h, add_done[t_ - 2])
            for k in range(8):
                sP.inc(MM(tgt[:, 0:512], lnT[:, k, t_ * 128:(t_ + 1) * 128], woT[:, k, 0, :],
                          start=(k == 0), stop=(k == 7)), 1)
            for k in range(8):
                sP.inc(MM(tgt[:, 512:1024], lnT[:, k, t_ * 128:(t_ + 1) * 128], woT[:, k, 1, :],
                          start=(k == 0), stop=(k == 7)), 1)
            ve.wait_ge(sP.h, sP.n)
            ve.wait_ge(sH.h, sH.n)
            if t_ >= 2:
                ve.wait_ge(sG.h, g5 + (t_ - 1) * 32)
            sV.inc(ve.tensor_add(x2t[:, kb, :], tgt[:, :], xa[:, kb, :]), 1)
            add_done.append(sV.n)
            g.wait_ge(sV.h, sV.n)
            sG.inc(g.dma_start(out=x2_d[xrow + t_ * 128:xrow + (t_ + 1) * 128, :], in_=x2t[:, kb, :]), 16)
            ac.wait_ge(sV.h, sV.n)
            sA.inc(ac.activation(scr[:, 0:1024], x2t[:, kb, :], AF.Identity, accum_out=stat[:, 0, 0:1]), 1)
            sA.inc(ac.activation(scr[:, 1024:2048], x2t[:, kb, :], AF.Square, accum_out=stat[:, 1, 0:1]), 1)
            ve.wait_ge(sA.h, sA.n)
            sV.inc(ve.tensor_scalar(stat[:, 2, 0:1], stat[:, 0, 0:1], -1.0 / D, None, ALU.mult), 1)
            sV.inc(ve.tensor_scalar(stat[:, 3, 0:1], stat[:, 1, 0:1], 1.0 / D, None, ALU.mult), 1)
            ve.wait_ge(sV.h, sV.n)
            sV.inc(ve.tensor_mul(stat[:, 4, 0:1], stat[:, 2, 0:1], stat[:, 2, 0:1]), 1)
            ve.wait_ge(sV.h, sV.n)
            sV.inc(ve.tensor_sub(stat[:, 5, 0:1], stat[:, 3, 0:1], stat[:, 4, 0:1]), 1)
            ac.wait_ge(sV.h, sV.n)
            sA.inc(ac.activation(stat[:, 6, 0:1], stat[:, 5, 0:1], AF.Ln, bias=1e-5), 1)
            ac.wait_ge(sA.h, sA.n)
            sA.inc(ac.activation(stat[:, 7, 0:1], stat[:, 6, 0:1], AF.Exp, scale=-0.5), 1)
            ve.wait_ge(sA.h, sA.n)
            sV.inc(ve.tensor_scalar(scf[:, :], x2t[:, kb, :], stat[:, 2, 0:1], None, ALU.add), 1)
            ve.wait_ge(sV.h, sV.n)
            sV.inc(ve.tensor_scalar(ln2t[:, kb, :], scf[:, :], stat[:, 7, 0:1], None, ALU.mult), 1)
            g.wait_ge(sV.h, sV.n)
            sG.inc(g.dma_start(out=ln2_d[xrow + t_ * 128:xrow + (t_ + 1) * 128, :], in_=ln2t[:, kb, :]), 16)
        g.wait_ge(sG.h, sG.n)
        nc.all_engine_barrier()
        for k in range(8):
            sH.inc(sy.dma_start(out=lnT[:, k, :], in_=ln2_d[xrow:xrow + 2048, k * 128:(k + 1) * 128],
                                transpose=True), 16)
        ve.wait_ge(sH.h, sH.n)
        nc.all_engine_barrier()
        for t_ in range(16):
            if t_ >= 1:
                ac.wait_ge(sA.h, sA.n)
                pe.wait_ge(sA.h, sA.n)
            for k in range(8):
                sP.inc(MM(psV[:, 0:8], lnT[:, k, t_ * 128:(t_ + 1) * 128], rwT[:, k, :],
                          start=(k == 0), stop=(k == 7)), 1)
            ac.wait_ge(sP.h, sP.n)
            sA.inc(ac.activation(lgst[:, t_, :], psV[:, 0:8], AF.Identity), 1)
        g.wait_ge(sA.h, sA.n)
        sG.inc(g.dma_start(out=lg_d[xrow:xrow + 2048, :].rearrange("(k p) e -> p k e", p=128),
                           in_=lgst[:, :, :]), 16)
        g.wait_ge(sG.h, sG.n)
        nc.all_engine_barrier()

    if dbg == "attn":
        dst = ExitStack()
        dmp = dst.enter_context(nc.sbuf_tensor("dmp", [128, 1024], F32))
        dmp2 = dst.enter_context(nc.sbuf_tensor("dmp2", [128, 8], F32))
        for r in range(TOK // 128):
            sy.wait_ge(sO.h, sO.n)
            sH.inc(sy.dma_start(out=dmp[:, :], in_=x2_d[r * 128:(r + 1) * 128, :]), 16)
            ve.wait_ge(sH.h, sH.n)
            sV.inc(ve.engine_nop(), 1)
            g.wait_ge(sV.h, sV.n)
            sO.inc(g.dma_start(out=dbg_outs["d_x2"][r * 128:(r + 1) * 128, :], in_=dmp[:, :]), 16)
            g.wait_ge(sO.h, sO.n)
            sy.wait_ge(sO.h, sO.n)
            sH.inc(sy.dma_start(out=dmp2[:, :], in_=lg_d[r * 128:(r + 1) * 128, :]), 16)
            ve.wait_ge(sH.h, sH.n)
            sV.inc(ve.engine_nop(), 1)
            g.wait_ge(sV.h, sV.n)
            sO.inc(g.dma_start(out=dbg_outs["d_lg"][r * 128:(r + 1) * 128, :], in_=dmp2[:, :]), 16)
            g.wait_ge(sO.h, sO.n)
        for r in range(16):
            sO.inc(g.dma_start(out=dbg_outs["d_qk"][r * 128:(r + 1) * 128, :],
                               in_=qk_d[r * 128:(r + 1) * 128, :]), 16)
            sO.inc(g.dma_start(out=dbg_outs["d_xbf"][r * 128:(r + 1) * 128, :],
                               in_=xbf_d[r * 128:(r + 1) * 128, :]), 16)
            g.wait_ge(sO.h, sO.n)
        for r in range(8):
            sO.inc(g.dma_start(out=dbg_outs["d_yt"][r * 128:(r + 1) * 128, :],
                               in_=yt_d[r * 128:(r + 1) * 128, :]), 16)
            g.wait_ge(sO.h, sO.n)
        sO.inc(g.dma_start(out=dbg_outs["d_vt"][:, :], in_=vt.ap().rearrange("p t f -> p (t f)")), 16)
        g.wait_ge(sO.h, sO.n)
        dst.close()
        att.close()
        return nc, dbg_outs

    att.close()

    g.wait_ge(sG.h, sG.n)
    sG.inc(g.dma_start(out=shLG[ds(r_xbase, TOK), :], in_=lg_d[:, :]), 16)
    g.wait_ge(sG.h, sG.n)
    sG.inc(g.dma_start(out=shFL[ds(r_par, 1), 0:1], in_=flagw[:, :]), 16)
    g.wait_ge(sG.h, sG.n)
    with g.register("notdone") as regnd:
        g.reg_mov(regnd, 1)
        with g.While(regnd):
            g.dma_start(out=flrd[:, :], in_=shFL[ds(1 - r_par, 1), 0:1]).then_inc(ms_poll.sem(), 16)
            ms_poll.wait_inc(16)
            vfl = g.value_load(flrd[0:1, 0:1])
            g.reg_alu(regnd, vfl, MAGIC, ALU.not_equal)
    nc.all_engine_barrier()

    rt = ExitStack()

    def sbr(name, shape, dt):
        return rt.enter_context(nc.sbuf_tensor(name, shape, dt))

    r_og2 = nc.values_load(prmt[0:1, 3:4], engines=mybir.ALL_ENGINES, min_val=0, max_val=256 - 32)
    r_ox2 = nc.values_load(prmt[0:1, 1:2], engines=mybir.ALL_ENGINES, min_val=0, max_val=TOK - 4096)
    lg = sbr("lg", [128, 256, 8], F32)
    rA = sbr("rA", [128, 2048], F32)
    rB = sbr("rB", [128, 2048], F32)
    rC = sbr("rC", [128, 2048], F32)
    rD = sbr("rD", [128, 2048], F32)
    r6 = sbr("r6", [128, 8, 256], F32)
    we = sbr("we", [128, 8, 512], F32)
    wGT = sbr("wGT", [128, 8, 512], F32)
    wEQ = sbr("wEQ", [128, 8, 512], F32)
    idsel = sbr("idsel", [128, 8, 512], F32)
    colx = sbr("colx", [128, 8], F32)
    rr = sbr("rr", [1, 16, 8], F32)
    thb = sbr("thb", [128, 8], F32)
    Ib = sbr("Ib", [128, 8], F32)
    w12o = sbr("w12o", [128, 64], F32)
    e12o = sbr("e12o", [128, 64], F32)
    idxo = sbr("idxo", [128, 64], F32)
    keep = sbr("keep", [128, 64], F32)
    ohE = sbr("ohE", [128, 64], F32)
    kpe = sbr("kpe", [128, 64], BF16)
    slotp = sbr("slotp", [128, 64], F32)
    gdf = sbr("gdf", [128, 64], F32)
    gacc = sbr("gacc", [128, 2, 64], F32)
    tmp1 = sbr("tmp1", [128, 64], F32)
    tmp2 = sbr("tmp2", [128, 64], F32)
    crow = sbr("crow", [1, 4, 64], F32)
    nrow = sbr("nrow", [1, 64], F32)

    (LO, HI, MID, CNT, GE, GM, D1, D2, M1, M2, TH, GG, TT_, QQ, ILO, IHI) = range(16)

    def R(i):
        return rr[0:1, i, :]

    def vstep(instr):
        sV.inc(instr, 1)
        ve.wait_ge(sV.h, sV.n)

    sH.inc(sy.dma_start(out=lg[:, :, :], in_=shLG.rearrange("(gg p) e -> p gg e", p=128)), 16)
    ve.wait_ge(sH.h, sH.n)

    lgf = lg.ap().rearrange("p gg e -> p (gg e)")

    def bc8(x):
        return x.unsqueeze(2).to_broadcast([128, 256, 8])

    def g3(x2d):
        return x2d.rearrange("p (gg e) -> p gg e", e=8)

    io8b = c_io8.ap().unsqueeze(1).to_broadcast([128, 256, 8])
    io8mb = c_io8m.ap().unsqueeze(1).to_broadcast([128, 256, 8])
    vstep(ve.tensor_reduce(r6[:, 0, :], lg[:, :, :], AX.X, ALU.max))
    vstep(ve.tensor_tensor(g3(rA[:, :]), lg[:, :, :], bc8(r6[:, 0, :]), ALU.subtract))
    ac.wait_ge(sV.h, sV.n)
    sA.inc(ac.activation(rB[:, :], rA[:, :], AF.Exp), 1)
    ve.wait_ge(sA.h, sA.n)
    vstep(ve.tensor_reduce(r6[:, 0, :], rB[:, :].rearrange("p (gg e) -> p gg e", e=8), AX.X, ALU.add))
    vstep(ve.reciprocal(r6[:, 1, :], r6[:, 0, :]))
    vstep(ve.tensor_tensor(g3(rA[:, :]), g3(rB[:, :]), bc8(r6[:, 1, :]), ALU.mult))
    vstep(ve.tensor_reduce(r6[:, 0, :], rA[:, :].rearrange("p (gg e) -> p gg e", e=8), AX.X, ALU.max))
    vstep(ve.tensor_tensor(g3(rB[:, :]), g3(rA[:, :]), bc8(r6[:, 0, :]), ALU.is_ge))
    vstep(ve.tensor_tensor(g3(rC[:, :]), g3(rB[:, :]), io8mb, ALU.mult))
    vstep(ve.tensor_scalar(rD[:, :], rC[:, :], 999.0, None, ALU.add))
    vstep(ve.tensor_reduce(r6[:, 1, :], rD[:, :].rearrange("p (gg e) -> p gg e", e=8), AX.X, ALU.min))
    vstep(ve.tensor_tensor(g3(rB[:, :]), io8b, bc8(r6[:, 1, :]), ALU.is_equal))
    vstep(ve.scalar_tensor_tensor(rC[:, :], rB[:, :], -2.0, rA[:, :], ALU.mult, ALU.add))
    vstep(ve.tensor_reduce(r6[:, 2, :], rC[:, :].rearrange("p (gg e) -> p gg e", e=8), AX.X, ALU.max))
    vstep(ve.tensor_tensor(g3(rB[:, :]), g3(rC[:, :]), bc8(r6[:, 2, :]), ALU.is_ge))
    vstep(ve.tensor_tensor(g3(rD[:, :]), g3(rB[:, :]), io8mb, ALU.mult))
    vstep(ve.tensor_scalar(rB[:, :], rD[:, :], 999.0, None, ALU.add))
    vstep(ve.tensor_reduce(r6[:, 3, :], rB[:, :].rearrange("p (gg e) -> p gg e", e=8), AX.X, ALU.min))
    vstep(ve.tensor_add(r6[:, 4, :], r6[:, 0, :], r6[:, 2, :]))
    vstep(ve.tensor_scalar(r6[:, 5, :], r6[:, 4, :], EPS, None, ALU.max))
    vstep(ve.reciprocal(r6[:, 6, :], r6[:, 5, :]))
    vstep(ve.tensor_tensor(r6[:, 4, :], r6[:, 0, :], r6[:, 6, :], ALU.mult))
    vstep(ve.tensor_tensor(r6[:, 5, :], r6[:, 2, :], r6[:, 6, :], ALU.mult))
    vstep(ve.tensor_copy(rA[:, 0:256], r6[:, 4, :]))
    vstep(ve.tensor_copy(rA[:, 256:512], r6[:, 5, :]))
    for e_ in range(E):
        vstep(ve.tensor_scalar(rB[:, 0:256], r6[:, 1, :], float(e_), None, ALU.is_equal))
        vstep(ve.tensor_scalar(rB[:, 256:512], r6[:, 3, :], float(e_), None, ALU.is_equal))
        vstep(ve.tensor_tensor(rC[:, 0:512], rB[:, 0:512], rA[:, 0:512], ALU.mult))
        vstep(ve.tensor_scalar(rD[:, 0:512], rB[:, 0:512], -1.0, None, ALU.add))
        vstep(ve.tensor_tensor(we[:, e_, :], rC[:, 0:512], rD[:, 0:512], ALU.add))

    wef = we.ap().rearrange("p e c -> p (e c)")
    wgf = wGT.ap().rearrange("p e c -> p (e c)")
    weqf = wEQ.ap().rearrange("p e c -> p (e c)")
    idf = idsel.ap().rearrange("p e c -> p (e c)")

    def bcth(lohalf):
        sl = slice(0, 4) if lohalf == 0 else slice(4, 8)
        return thb.ap()[:, sl].unsqueeze(2).to_broadcast([128, 4, 512])

    def e3(x2d, ne=4):
        return x2d.rearrange("p (e c) -> p e c", c=512)

    def count8(srcf, cmp_op, thbuf=True):
        vstep(ve.tensor_tensor(e3(rA[:, 0:2048]), e3(srcf[:, 0:2048]), bcth(0), cmp_op))
        vstep(ve.tensor_tensor(e3(rB[:, 0:2048]), e3(srcf[:, 2048:4096]), bcth(1), cmp_op))
        vstep(ve.tensor_reduce(colx[:, 0:4], rA[:, 0:2048].rearrange("p (e c) -> p e c", c=512),
                               AX.X, ALU.add))
        vstep(ve.tensor_reduce(colx[:, 4:8], rB[:, 0:2048].rearrange("p (e c) -> p e c", c=512),
                               AX.X, ALU.add))
        pe.wait_ge(sV.h, sV.n)
        sP.inc(MM(psV[0:1, 0:8], c_onescf[:, :], colx[:, :], start=True, stop=True), 1)
        ve.wait_ge(sP.h, sP.n)

    def bcrow(src_row):
        pe.wait_ge(sV.h, sV.n)
        sP.inc(MM(psW[:, 0:8], c_onesrf[0:1, :], src_row, start=True, stop=True), 1)
        ve.wait_ge(sP.h, sP.n)
        vstep(ve.tensor_copy(thb[:, :], psW[:, 0:8]))

    vstep(ve.memset(R(LO), -1.0))
    vstep(ve.memset(R(HI), 1.5))
    for rnd in range(VAL_ROUNDS):
        vstep(ve.tensor_add(R(MID), R(LO), R(HI)))
        vstep(ve.tensor_scalar(R(D1), R(MID), 0.5, None, ALU.mult))
        bcrow(R(D1))
        count8(wef, ALU.is_gt)
        vstep(ve.tensor_copy(R(CNT), psV[0:1, 0:8]))
        vstep(ve.tensor_scalar(R(GE), R(CNT), CAP, None, ALU.is_ge))
        vstep(ve.tensor_sub(R(D2), R(D1), R(LO)))
        vstep(ve.tensor_mul(R(M1), R(GE), R(D2)))
        vstep(ve.tensor_add(R(M2), R(LO), R(M1)))
        vstep(ve.tensor_sub(R(D2), R(D1), R(HI)))
        vstep(ve.tensor_scalar(R(GM), R(GE), -1.0, None, ALU.add))
        vstep(ve.tensor_mul(R(M1), R(GM), R(D2)))
        vstep(ve.tensor_sub(R(D1), R(HI), R(M1)))
        vstep(ve.tensor_copy(R(LO), R(M2)))
        vstep(ve.tensor_copy(R(HI), R(D1)))
    bcrow(R(LO))
    vstep(ve.tensor_tensor(wGT[:, :, :], we[:, :, :], thb.ap().unsqueeze(2).to_broadcast([128, 8, 512]),
                           ALU.is_gt))
    vstep(ve.tensor_scalar(rC[:, 0:2048], wgf[:, 0:2048], -3.0, None, ALU.mult))
    vstep(ve.tensor_scalar(rD[:, 0:2048], wgf[:, 2048:4096], -3.0, None, ALU.mult))
    vstep(ve.tensor_scalar(rA[:, 0:2048], rC[:, 0:2048], 3.0, None, ALU.add))
    vstep(ve.tensor_scalar(rB[:, 0:2048], rD[:, 0:2048], 3.0, None, ALU.add))
    vstep(ve.tensor_tensor(rC[:, 0:2048], rA[:, 0:2048], wef[:, 0:2048], ALU.add))
    vstep(ve.tensor_tensor(rD[:, 0:2048], rB[:, 0:2048], wef[:, 2048:4096], ALU.add))
    vstep(ve.tensor_reduce(colx[:, 0:4], rC[:, 0:2048].rearrange("p (e c) -> p e c", c=512),
                           AX.X, ALU.min))
    vstep(ve.tensor_reduce(colx[:, 4:8], rD[:, 0:2048].rearrange("p (e c) -> p e c", c=512),
                           AX.X, ALU.min))
    pe.wait_ge(sV.h, sV.n)
    sP.inc(pe.transpose(psW[0:8, 0:128], colx[:, :], c_identf[:, :]), 1)
    ve.wait_ge(sP.h, sP.n)
    vstep(ve.tensor_reduce(colx[0:8, 0:1], psW[0:8, 0:128], AX.X, ALU.min))
    pe.wait_ge(sV.h, sV.n)
    sP.inc(pe.transpose(psW[0:1, 0:8], colx[0:8, 0:1], c_identf[0:8, 0:8]), 1)
    ve.wait_ge(sP.h, sP.n)
    vstep(ve.tensor_copy(R(TH), psW[0:1, 0:8]))
    bcrow(R(TH))
    vstep(ve.tensor_tensor(wGT[:, :, :], we[:, :, :], thb.ap().unsqueeze(2).to_broadcast([128, 8, 512]),
                           ALU.is_gt))
    vstep(ve.tensor_tensor(wEQ[:, :, :], we[:, :, :], thb.ap().unsqueeze(2).to_broadcast([128, 8, 512]),
                           ALU.is_equal))
    vstep(ve.tensor_reduce(colx[:, :], wGT[:, :, :], AX.X, ALU.add))
    pe.wait_ge(sV.h, sV.n)
    sP.inc(MM(psV[0:1, 0:8], c_onescf[:, :], colx[:, :], start=True, stop=True), 1)
    ve.wait_ge(sP.h, sP.n)
    vstep(ve.tensor_copy(R(GG), psV[0:1, 0:8]))
    vstep(ve.tensor_reduce(colx[:, :], wEQ[:, :, :], AX.X, ALU.add))
    pe.wait_ge(sV.h, sV.n)
    sP.inc(MM(psV[0:1, 0:8], c_onescf[:, :], colx[:, :], start=True, stop=True), 1)
    ve.wait_ge(sP.h, sP.n)
    vstep(ve.tensor_copy(R(TT_), psV[0:1, 0:8]))
    vstep(ve.tensor_scalar(R(M1), R(GG), -1.0, None, ALU.mult))
    vstep(ve.tensor_scalar(R(M2), R(M1), CAP, None, ALU.add))
    vstep(ve.tensor_tensor(R(QQ), R(M2), R(TT_), ALU.min))
    idcmb4 = c_idcm.ap().unsqueeze(1).to_broadcast([128, 4, 512])
    vstep(ve.tensor_tensor(e3(rA[:, 0:2048]), wEQ[:, 0:4, :], idcmb4, ALU.mult))
    vstep(ve.tensor_tensor(e3(rB[:, 0:2048]), wEQ[:, 4:8, :], idcmb4, ALU.mult))
    vstep(ve.tensor_scalar(idf[:, 0:2048], rA[:, 0:2048], 131072.0, None, ALU.add))
    vstep(ve.tensor_scalar(idf[:, 2048:4096], rB[:, 0:2048], 131072.0, None, ALU.add))
    vstep(ve.memset(R(ILO), -0.5))
    vstep(ve.memset(R(IHI), 65535.5))
    for rnd in range(IDX_ROUNDS):
        vstep(ve.tensor_add(R(MID), R(ILO), R(IHI)))
        vstep(ve.tensor_scalar(R(D1), R(MID), 0.5, None, ALU.mult))
        bcrow(R(D1))
        count8(idf, ALU.is_le)
        vstep(ve.tensor_copy(R(CNT), psV[0:1, 0:8]))
        vstep(ve.tensor_tensor(R(GE), R(CNT), R(QQ), ALU.is_ge))
        vstep(ve.tensor_sub(R(D2), R(D1), R(IHI)))
        vstep(ve.tensor_mul(R(M1), R(GE), R(D2)))
        vstep(ve.tensor_add(R(M2), R(IHI), R(M1)))
        vstep(ve.tensor_sub(R(D2), R(D1), R(ILO)))
        vstep(ve.tensor_scalar(R(GM), R(GE), -1.0, None, ALU.add))
        vstep(ve.tensor_mul(R(M1), R(GM), R(D2)))
        vstep(ve.tensor_sub(R(D1), R(ILO), R(M1)))
        vstep(ve.tensor_copy(R(IHI), R(M2)))
        vstep(ve.tensor_copy(R(ILO), R(D1)))
    bcrow(R(TH))
    pe.wait_ge(sV.h, sV.n)
    sP.inc(MM(psW[:, 0:8], c_onesrf[0:1, :], R(IHI), start=True, stop=True), 1)
    ve.wait_ge(sP.h, sP.n)
    vstep(ve.tensor_copy(Ib[:, :], psW[:, 0:8]))

    rQ = rD.ap()[:, 0:1024].rearrange("p (gl q) -> p gl q", q=4)
    for jj, srcap in ((0, r6[:, 4, :]), (1, r6[:, 5, :]), (2, r6[:, 1, :]), (3, r6[:, 3, :])):
        vstep(ve.tensor_copy(rQ[:, :, jj], srcap))
    g.wait_ge(sV.h, sV.n)
    sG.inc(g.dma_start(out=rt_d[:, :, :], in_=rQ), 16)
    g.wait_ge(sG.h, sG.n)
    sG.inc(g.dma_start(out=tmp1.ap().rearrange("p (gl q) -> p gl q", q=2).unsqueeze(3) if False else
                       gacc[:, :, :].rearrange("p a c -> p (a c)").rearrange("p (gl q) -> p gl q", q=4),
                       in_=rt_d[:, ds(r_og2, 32), :]), 16)
    g.wait_ge(sG.h, sG.n)
    ve.wait_ge(sG.h, sG.n)
    gq = gacc.ap().rearrange("p a c -> p (a c)").rearrange("p (gl q) -> p gl q", q=4)
    vstep(ve.tensor_copy(w12o[:, 0:32], gq[:, :, 0]))
    vstep(ve.tensor_copy(w12o[:, 32:64], gq[:, :, 1]))
    vstep(ve.tensor_copy(e12o[:, 0:32], gq[:, :, 2]))
    vstep(ve.tensor_copy(e12o[:, 32:64], gq[:, :, 3]))
    pe.wait_ge(sV.h, sV.n)
    sP.inc(MM(psV[:, 0:1], c_onesrf[0:1, :], prmft[0:1, 1:2], start=True, stop=True), 1)
    ve.wait_ge(sP.h, sP.n)
    vstep(ve.tensor_copy(tmp2[:, 0:1], psV[:, 0:1]))
    vstep(ve.tensor_scalar(idxo[:, :], c_idxp[:, :], tmp2[:, 0:1], None, ALU.add))
    vstep(ve.memset(gacc[:, 0, :], 0.0))
    vstep(ve.memset(gacc[:, 1, :], 0.0))
    for e_ in range(E):
        vstep(ve.tensor_scalar(ohE[:, :], e12o[:, :], float(e_), None, ALU.is_equal))
        vstep(ve.tensor_tensor(tmp1[:, :], ohE[:, :],
                               thb.ap()[:, e_:e_ + 1].to_broadcast([128, 64]), ALU.mult))
        vstep(ve.tensor_add(tmp2[:, :], gacc[:, 0, :], tmp1[:, :]))
        vstep(ve.tensor_copy(gacc[:, 0, :], tmp2[:, :]))
        vstep(ve.tensor_tensor(tmp1[:, :], ohE[:, :],
                               Ib.ap()[:, e_:e_ + 1].to_broadcast([128, 64]), ALU.mult))
        vstep(ve.tensor_add(tmp2[:, :], gacc[:, 1, :], tmp1[:, :]))
        vstep(ve.tensor_copy(gacc[:, 1, :], tmp2[:, :]))
    vstep(ve.tensor_tensor(tmp1[:, :], w12o[:, :], gacc[:, 0, :], ALU.is_gt))
    vstep(ve.tensor_tensor(tmp2[:, :], w12o[:, :], gacc[:, 0, :], ALU.is_equal))
    vstep(ve.tensor_tensor(slotp[:, :], idxo[:, :], gacc[:, 1, :], ALU.is_le))
    vstep(ve.tensor_tensor(gdf[:, :], tmp2[:, :], slotp[:, :], ALU.mult))
    vstep(ve.tensor_max(keep[:, :], tmp1[:, :], gdf[:, :]))
    vstep(ve.tensor_mul(tmp1[:, :], w12o[:, :], keep[:, :]))
    vstep(ve.tensor_add(tmp2[:, 0:32], tmp1[:, 0:32], tmp1[:, 32:64]))
    vstep(ve.tensor_scalar(tmp2[:, 32:64], tmp2[:, 0:32], EPS, None, ALU.max))
    vstep(ve.reciprocal(slotp[:, 0:32], tmp2[:, 32:64]))
    vstep(ve.tensor_tensor(kwsb.ap().rearrange("p (a bb) -> p a bb", a=2),
                           tmp1.ap().rearrange("p (a bb) -> p a bb", a=2),
                           slotp[:, 0:32].unsqueeze(1).to_broadcast([128, 2, 32]),
                           ALU.mult))
    vstep(ve.memset(gdf[:, :], 0.0))
    for e_ in range(E):
        vstep(ve.tensor_scalar(ohE[:, :], e12o[:, :], float(e_), None, ALU.is_equal))
        vstep(ve.tensor_mul(tmp1[:, :], ohE[:, :], keep[:, :]))
        vstep(ve.tensor_copy(kpe[:, :], tmp1[:, :]))
        pe.wait_ge(sV.h, sV.n)
        sP.inc(MM(psW[:, 0:64], c_tri[:, :], kpe[:, :], start=True, stop=True), 1)
        sP.inc(MM(psV[0:1, 0:64], onescol[:, :], kpe[:, :], start=True, stop=True), 1)
        ve.wait_ge(sP.h, sP.n)
        vstep(ve.tensor_copy(slotp[:, :], psW[:, 0:64]))
        vstep(ve.tensor_copy(crow[0:1, 0, :], psV[0:1, 0:64]))
        vstep(ve.tensor_copy(crow[0:1, 2, :], psV[0:1, 0:64]))
        for sh in (1, 2, 4, 8, 16, 32):
            vstep(ve.tensor_add(crow[0:1, 1, sh:64], crow[0:1, 0, sh:64], crow[0:1, 0, 0:64 - sh]))
            vstep(ve.tensor_copy(crow[0:1, 1, 0:sh], crow[0:1, 0, 0:sh]))
            vstep(ve.tensor_copy(crow[0:1, 0, :], crow[0:1, 1, :]))
        vstep(ve.tensor_copy(nrow[0:1, e_:e_ + 1], crow[0:1, 0, 63:64]))
        vstep(ve.tensor_sub(crow[0:1, 1, :], crow[0:1, 0, :], crow[0:1, 2, :]))
        pe.wait_ge(sV.h, sV.n)
        sP.inc(MM(psW[:, 0:64], c_onesrf[0:1, :], crow[0:1, 1, :], start=True, stop=True), 1)
        ve.wait_ge(sP.h, sP.n)
        vstep(ve.tensor_tensor(tmp1[:, :], slotp[:, :], psW[:, 0:64], ALU.add))
        vstep(ve.tensor_scalar(tmp2[:, :], tmp1[:, :], float(e_ * BUCKET), None, ALU.add))
        vstep(ve.tensor_tensor(tmp1[:, :], tmp2[:, :], kpe[:, :], ALU.mult))
        vstep(ve.tensor_add(tmp2[:, :], gdf[:, :], tmp1[:, :]))
        vstep(ve.tensor_copy(gdf[:, :], tmp2[:, :]))
    vstep(ve.tensor_scalar(tmp1[:, :], keep[:, :], float(-SENT), None, ALU.mult))
    vstep(ve.tensor_scalar(tmp2[:, :], tmp1[:, :], float(SENT), None, ALU.add))
    vstep(ve.tensor_add(tmp1[:, :], gdf[:, :], tmp2[:, :]))
    vstep(ve.tensor_copy(gdU[:, :], tmp1[:, :]))
    vstep(ve.memset(crow[0:1, 0, 0:8], 0.0))
    for j_ in range(8):
        vstep(ve.tensor_scalar(crow[0:1, 1, 0:8], nrow[0:1, 0:8], float(j_ * CHUNK), None, ALU.is_gt))
        vstep(ve.tensor_add(crow[0:1, 2, 0:8], crow[0:1, 0, 0:8], crow[0:1, 1, 0:8]))
        vstep(ve.tensor_copy(crow[0:1, 0, 0:8], crow[0:1, 2, 0:8]))
    vstep(ve.tensor_copy(kcu[0:1, :], crow[0:1, 0, 0:8]))

    if dbg == "route":
        g.wait_ge(sV.h, sV.n)
        sO.inc(g.dma_start(out=dbg_outs["d_rt"][:, 0:4096], in_=wef), 16)
        sO.inc(g.dma_start(out=dbg_outs["d_rt"][:, 4096:4160], in_=w12o[:, :]), 16)
        sO.inc(g.dma_start(out=dbg_outs["d_rt"][:, 4160:4224], in_=e12o[:, :]), 16)
        sO.inc(g.dma_start(out=dbg_outs["d_rt"][:, 4224:4288], in_=keep[:, :]), 16)
        sO.inc(g.dma_start(out=dbg_outs["d_rt"][:, 4288:4352], in_=kwsb[:, :]), 16)
        sO.inc(g.dma_start(out=dbg_outs["d_rt"][:, 4352:4416], in_=gdU[:, :].bitcast(F32)), 16)
        sO.inc(g.dma_start(out=dbg_outs["d_rt"][:, 4416:4480], in_=gacc[:, 0, :]), 16)
        sO.inc(g.dma_start(out=dbg_outs["d_rt"][:, 4480:4544], in_=gacc[:, 1, :]), 16)
        sO.inc(g.dma_start(out=dbg_outs["d_rt"][:, 4544:4608], in_=idxo[:, :]), 16)
        sO.inc(g.dma_start(out=dbg_outs["d_lgg"][:, :], in_=lgf), 16)
        for i_, rowidx in enumerate((TH, IHI, GG, TT_, QQ)):
            sO.inc(g.dma_start(out=dbg_outs["d_row"][i_:i_ + 1, 0:8], in_=R(rowidx)), 16)
        sO.inc(g.dma_start(out=dbg_outs["d_row"][5:6, 0:64], in_=nrow[:, :]), 16)
        sO.inc(g.dma_start(out=dbg_outs["d_row"][6:7, 0:8], in_=kcu[0:1, :].bitcast(F32)), 16)
        g.wait_ge(sO.h, sO.n)
        rt.close()
        return nc, dbg_outs

    g.wait_ge(sV.h, sV.n)
    sy.wait_ge(sV.h, sV.n)
    rt.close()
    sc_stk = ExitStack()
    scL = sc_stk.enter_context(nc.sbuf_tensor("scL", [128, 32, 1024], BF16))
    sH.inc(sy.dma_start(out=scL[:, :, :],
                        in_=ln2_d[ds(r_ox2, 4096), :].rearrange("(gl p) d -> p gl d", p=128)), 16)
    sG.inc(g.dma_start(out=x2o_d[:, :], in_=x2_d[ds(r_ox2, 4096), :]), 16)
    g.wait_ge(sH.h, sH.n)
    g.wait_ge(sG.h, sG.n)
    for c in range(64):
        gl = c % 32
        sG.inc(g.indirect_dma_start(out=bkt_d[:, :],
                                    out_offset=IndirectOffsetOnAxis(gdU[:, c:c + 1], 0),
                                    in_=scL[:, gl, :],
                                    in_offset=None), 16)
    g.wait_ge(sG.h, sG.n)
    sc_stk.close()
    rt.close()
    nc.all_engine_barrier()

    moe = ExitStack()

    def sbm(name, shape, dt):
        return moe.enter_context(nc.sbuf_tensor(name, shape, dt))

    hT = sbm("hT", [128, 8, CHUNK], BF16)
    actb = sbm("actb", [128, 32, CHUNK], BF16)
    wAB = sbm("wAB", [128, 3, 2, 8, 128], BF16)
    wGB = sbm("wGB", [128, 3, 2, 8, 128], BF16)
    w2B = sbm("w2B", [128, 2, 16, 512], BF16)
    siS = sbm("siS", [128, 2, 512], BF16)
    outT = sbm("outT", [128, 8, 512], BF16)
    tokS = sbm("tokS", [128, 4, 1024], BF16)

    from concourse.expressions import RuntimeValue
    moe_loop_regs = nc.alloc_registers("moeloop", engines=mybir.ALL_ENGINES)
    moe_ke_regs = nc.alloc_registers("moeke", engines=mybir.ALL_ENGINES)
    rbase_reg = g.alloc_register("moerbase")
    for e_c in range(E):
        for _eng in (g, sy, pe, ve, ac):
            _eng.reg_load(moe_ke_regs[_eng.engine], kcu[0:1, e_c:e_c + 1])
        ke_v = RuntimeValue(moe_ke_regs, min_val=0, max_val=8)
        nc.regs_mov(moe_loop_regs, 0)
        lstart, lend = f"moe{e_c}_loop", f"moe{e_c}_end"
        nc.br_eq(moe_ke_regs, 0, on_true=lend, on_false=lstart, engines=mybir.ALL_ENGINES)
        with nc.body(lstart, valid_engines=mybir.ALL_ENGINES):
            g.reg_alu(rbase_reg, moe_loop_regs[g.engine], CHUNK, ALU.mult)
            g.reg_alu(rbase_reg, rbase_reg, e_c * BUCKET, ALU.add)
            rbase = RuntimeValue(rbase_reg, min_val=e_c * BUCKET, max_val=e_c * BUCKET + (8 - 1) * CHUNK)
            def wpair(buf, dp):
                for dd in range(2):
                    d_ = dp * 2 + dd
                    g.dma_start(out=wAB[:, buf, dd, :, :],
                                in_=w1t[e_c * D:(e_c + 1) * D, d_ * 128:(d_ + 1) * 128].rearrange("(k p) f -> p k f", p=128)
                                ).then_inc(ms_wt.sem(), 16)
                    g.dma_start(out=wGB[:, buf, dd, :, :],
                                in_=wgt[e_c * D:(e_c + 1) * D, d_ * 128:(d_ + 1) * 128].rearrange("(k p) f -> p k f", p=128)
                                ).then_inc(ms_wt.sem(), 16)

            g.dma_start(out=chk_d[:, :], in_=bkt_d[ds(rbase, CHUNK), :]).then_inc(ms_mg.sem(), 16)
            wpair(0, 0)
            wpair(1, 1)
            ms_mg.wait_inc(16)
            ms_wt.wait_inc(64)
            nc.all_engine_barrier()
            for k in range(8):
                sy.dma_start(out=hT[:, k, :], in_=chk_d[:, k * 128:(k + 1) * 128],
                             transpose=True).then_inc(ms_mv.sem(), 16)
            ms_mv.wait_inc(16 * 8)
            nc.all_engine_barrier()

            def aslot(s):
                return psA[:, (s % 2) * 1024:(s % 2) * 1024 + 1024]

            def acons(s):
                sb = aslot(s)
                ac.activation(siS[:, s % 2, :], sb[:, 512:1024],
                              AF.Silu).then_inc(ms_mv2.sem(), 1)
                ms_mv2.wait_inc(1)
                ve.tensor_tensor(actb[:, s, :], sb[:, 0:512], siS[:, s % 2, :], ALU.mult)

            for s in range(32):
                if s >= 1:
                    acons(s - 1)
                sb = aslot(s)
                for k in range(8):
                    MM(sb[:, 0:512], wAB[:, (s // 2) % 3, s % 2, k, :], hT[:, k, :],
                       start=(k == 0), stop=(k == 7))
                    MM(sb[:, 512:1024], wGB[:, (s // 2) % 3, s % 2, k, :], hT[:, k, :],
                       start=(k == 0), stop=(k == 7))
                if s % 2 == 0 and s // 2 + 2 < 16:
                    wpair((s // 2 + 2) % 3, s // 2 + 2)
                if s % 2 == 1 and (s + 1) // 2 < 16:
                    ms_wt.wait_inc(64)
                nc.all_engine_barrier()
            acons(31)
            for half in range(2):
                for part in range(2):
                    g.dma_start(out=w2B[:, part, :, :],
                                in_=w2t[e_c * DFF + part * 2048:e_c * DFF + (part + 1) * 2048,
                                        half * 512:(half + 1) * 512].rearrange("(k p) f -> p k f", p=128)
                                ).then_inc(ms_mg.sem(), 16)
                ms_mg.wait_inc(16)
                nc.all_engine_barrier()
                for part in range(2):
                    if part == 1:
                        ms_mg.wait_inc(16)
                        nc.all_engine_barrier()
                    for dl in range(16):
                        d_ = part * 16 + dl
                        for cc in range(4):
                            mmb = MM(psA[:, cc * 512:(cc + 1) * 512], w2B[:, part, dl, cc * 128:(cc + 1) * 128],
                                     actb[:, d_, :], start=(d_ == 0), stop=(d_ == 31))
                            if d_ == 31 and cc == 3:
                                mmb.then_inc(ms_mv.sem(), 1)
                ms_mv.wait_inc(1)
                nc.all_engine_barrier()
                for cc in range(4):
                    evb = ac.activation(outT[:, half * 4 + cc, :], psA[:, cc * 512:(cc + 1) * 512],
                                        AF.Identity)
                    if cc == 3:
                        evb.then_inc(ms_mv2.sem(), 1)
                ms_mv2.wait_inc(1)
                nc.all_engine_barrier()
            for sb_ in range(4):
                for c8 in range(8):
                    tr = pe.transpose(psT[:, c8 * 128:(c8 + 1) * 128],
                                      outT[:, c8, sb_ * 128:(sb_ + 1) * 128], c_ident[:, :])
                    if c8 == 7:
                        tr.then_inc(ms_mv.sem(), 1)
                ms_mv.wait_inc(1)
                for c8 in range(8):
                    ve.tensor_copy(tokS[:, sb_, c8 * 128:(c8 + 1) * 128],
                                   psT[:, c8 * 128:(c8 + 1) * 128])
                nc.all_engine_barrier()
            g.dma_start(out=ob_d[ds(rbase, CHUNK), :].rearrange("(bl p) dd -> p bl dd", p=128),
                        in_=tokS[:, :, :]).then_inc(ms_mg.sem(), 16)
            ms_mg.wait_inc(16)
            nc.all_engine_barrier()
            nc.regs_alu(moe_loop_regs, moe_loop_regs, 1, op=ALU.add)
            nc.br_lt(moe_loop_regs, ke_v, on_true=lstart, on_false=lend, engines=mybir.ALL_ENGINES)
        nc.switch_bb(lend)
    moe.close()
    nc.all_engine_barrier()

    if dbg == "moe":
        for e_ in range(8):
            sO.inc(g.dma_start(out=dbg_outs["d_ob"][e_ * 1536:(e_ + 1) * 1536, :],
                               in_=ob_d[e_ * BUCKET:e_ * BUCKET + 1536, :]), 16)
            sO.inc(g.dma_start(out=dbg_outs["d_bk"][e_ * 1536:(e_ + 1) * 1536, :],
                               in_=bkt_d[e_ * BUCKET:e_ * BUCKET + 1536, :]), 16)
            g.wait_ge(sO.h, sO.n)
        sO.inc(g.dma_start(out=dbg_outs["d_lgg"][:, :], in_=shLG[0:TALL, :]), 16)
        g.wait_ge(sO.h, sO.n)
        return nc, dbg_outs

    fin_ = ExitStack()
    gb1 = fin_.enter_context(nc.sbuf_tensor("gb1", [128, 2, 1024], BF16))
    gb2 = fin_.enter_context(nc.sbuf_tensor("gb2", [128, 2, 1024], BF16))
    xf = fin_.enter_context(nc.sbuf_tensor("xf", [128, 2, 1024], F32))
    of1 = fin_.enter_context(nc.sbuf_tensor("of1", [128, 1024], F32))
    of2 = fin_.enter_context(nc.sbuf_tensor("of2", [128, 2, 1024], F32))

    for ct in range(32):
        cb = ct % 2
        if ct >= 2:
            g.wait_ge(sV.h, sV.n)
        sG.inc(g.indirect_dma_start(out=gb1[:, cb, :], out_offset=None,
                                    in_=ob_d[:, :],
                                    in_offset=IndirectOffsetOnAxis(gdU[:, ct:ct + 1], 0)), 16)
        sG.inc(g.indirect_dma_start(out=gb2[:, cb, :], out_offset=None,
                                    in_=ob_d[:, :],
                                    in_offset=IndirectOffsetOnAxis(gdU[:, 32 + ct:33 + ct], 0)), 16)
        if ct >= 2:
            sy.wait_ge(sV.h, sV.n)
        sH.inc(sy.dma_start(out=xf[:, cb, :],
                            in_=x2o_d[ct * 128:(ct + 1) * 128, :]), 16)
        ve.wait_ge(sG.h, sG.n)
        ve.wait_ge(sH.h, sH.n)
        ve.wait_ge(sV.h, sV.n)
        sV.inc(ve.scalar_tensor_tensor(of1[:, :], gb1[:, cb, :], kwsb[:, ct:ct + 1],
                                       xf[:, cb, :], ALU.mult, ALU.add), 1)
        ve.wait_ge(sV.h, sV.n)
        if ct >= 2:
            ve.wait_ge(sO.h, sO.n - 16)
        sV.inc(ve.scalar_tensor_tensor(of2[:, cb, :], gb2[:, cb, :], kwsb[:, 32 + ct:33 + ct],
                                       of1[:, :], ALU.mult, ALU.add), 1)
        g.wait_ge(sV.h, sV.n)
        sO.inc(g.dma_start(out=out[ct * 128:(ct + 1) * 128, :], in_=of2[:, cb, :]), 16)
    g.wait_ge(sO.h, sO.n)
    fin_.close()
    return nc, dbg_outs


def _host_prep(inputs):
    x = np.asarray(inputs["L0_attn_x"], dtype=np.float32)
    wqkv = np.ascontiguousarray(np.asarray(inputs["L0_attn_W_qkv"], dtype=np.float32))
    wo = np.ascontiguousarray(np.asarray(inputs["L0_attn_W_o"], dtype=np.float32))
    rw = np.ascontiguousarray(np.asarray(inputs["L0_ffn_router_weight"], dtype=np.float32))
    w1 = np.asarray(inputs["L0_ffn_ff_pre_act_weight"], dtype=np.float32)
    wg = np.asarray(inputs["L0_ffn_gate_weight"], dtype=np.float32)
    w2 = np.asarray(inputs["L0_ffn_ff_post_act_weight"], dtype=np.float32)
    w1t = np.ascontiguousarray(w1.transpose(0, 2, 1)).reshape(E * D, DFF)
    wgt = np.ascontiguousarray(wg.transpose(0, 2, 1)).reshape(E * D, DFF)
    w2t = np.ascontiguousarray(w2.transpose(0, 2, 1)).reshape(E * DFF, D)
    in_maps = []
    for c in range(NCORE):
        kk = c // 2
        if c % 2 == 0:
            xh = np.ascontiguousarray(x[0:8].reshape(TOK, D))
            xbase = 0
            obase = kk * 4096
        else:
            xh = np.ascontiguousarray(x[8:16].reshape(TOK, D))
            xbase = TOK
            obase = TOK + kk * 4096
        prmv = np.array([[xbase, obase - xbase, c % 2, obase // 128]], dtype=np.uint32)
        prmfv = np.array([[float(obase // 128), 256.0 * (obase // 128), 0.0, 0.0]], dtype=np.float32)
        in_maps.append({"xh": xh, "wqkv": wqkv, "wo": wo, "rw": rw,
                       "w1t": w1t, "wgt": wgt, "w2t": w2t, "prm": prmv, "prmf": prmfv})
    return in_maps


def kernel(**inputs):
    in_maps = _host_prep(inputs)
    nc, _ = build(dbg=DEBUG_STAGE)
    lower_extended_insts(nc)
    res = run_bass_kernel_spmd(nc, in_maps, core_ids=list(range(NCORE)))
    if DEBUG_STAGE is not None:
        return res
    outf = np.zeros((B, T, D), dtype=np.float32)
    for c in range(NCORE):
        o = np.asarray(res.results[c]["out"]).reshape(2, T, D)
        kk = c // 2
        if c % 2 == 0:
            outf[2 * kk] = o[0]
            outf[2 * kk + 1] = o[1]
        else:
            outf[8 + 2 * kk] = o[0]
            outf[8 + 2 * kk + 1] = o[1]
    return outf
